# version 05
# baseline (speedup 1.0000x reference)
import math
import os
from contextlib import ExitStack

_PHASES = int(os.environ.get("KPHASES", "5"))
_KP5 = int(os.environ.get("KP5", "9"))

import numpy as np
import ml_dtypes

import concourse.bass as bass
import concourse.mybir as mybir
from concourse.bass_utils import run_bass_kernel_spmd
from concourse.library_config import attn as _ATTN_LIB
from concourse.bass_isa import ReduceOp

F32 = mybir.dt.float32
BF16 = mybir.dt.bfloat16
FP16 = mybir.dt.float16
FP8 = mybir.dt.float8e4
AF = mybir.ActivationFunctionType
ALU = mybir.AluOpType
AX = mybir.AxisListType.X

D = 2048
T = 4096
NHC = 4
HID = 8192
CH = 512
NCH = T // CH
NCI = D // 128
QKVO = 3 * NHC * 128
NHI = HID // 128
ISD = 1.0 / math.sqrt(128.0)
MAGIC = 1536.0
EPS = 1e-5
NW = float(HID * D)
RG = [[0, 1, 2, 3, 4, 5, 6, 7]]

bfdt = ml_dtypes.bfloat16
ENGS = ("sync", "gpsimd", "scalar", "vector", "tensor")

R_MEAN, R_EX2, R_R, R_MR, R_T1, R_T2, R_T3 = 0, 512, 1024, 1536, 2048, 2560, 3072
R_SCA = 3584
R_GAM = 3600
R_SCO = 3608
R_SCO0, R_SCO1, R_QB0, R_QB1 = 0, 512, 1024, 1536


class Prog:

    def __init__(self, nc):
        self.nc = nc
        self.streams = {e: [] for e in ENGS}
        self.counts = {}
        self.sems = {}
        self.waited = {e: {} for e in ENGS}

    def sem(self, stack, name):
        self.sems[name] = stack.enter_context(self.nc.semaphore(name))
        self.counts[name] = 0

    def op(self, eng, fn, waits=(), inc=None, inc_k=1):
        w = []
        for s, c in waits:
            if c is None or c <= 0:
                continue
            if self.waited[eng].get(s, -1) >= c:
                continue
            self.waited[eng][s] = c
            w.append((s, c))
        if inc is not None:
            self.counts[inc] += inc_k
        self.streams[eng].append((fn, w, inc, inc_k))

    def wait(self, eng, sem, count=None):
        c = self.counts[sem] if count is None else count
        self.op(eng, None, waits=((sem, c),))

    def n(self, sem):
        return self.counts[sem]

    def barrier(self):
        for e in ENGS:
            self.op(e, None, inc="BAR")
        tgt = self.counts["BAR"]
        for e in ENGS:
            self.wait(e, "BAR", tgt)

    def replay(self, blk):
        for eng_name in ENGS:
            stream = self.streams[eng_name]

            def make(stream=stream):
                def run(eng):
                    for fn, waits, inc, inc_k in stream:
                        for s, c in waits:
                            eng.wait_ge(self.sems[s], c)
                        if fn is None:
                            if inc is not None:
                                eng.sem_inc(self.sems[inc], inc_k)
                            continue
                        ins = fn(eng)
                        if inc is not None:
                            ins.then_inc(self.sems[inc], inc_k)
                return run
            getattr(blk, eng_name)(make())


def _causal_masks():
    m = np.zeros((4, 128, CH), dtype=np.float32)
    for rel in range(4):
        for p in range(128):
            m[rel, p, 128 * rel + p:] = 1.0
    return m


def build_core(debug_taps=()):
    nc = bass.Bass("TRN2", target_bir_lowering=False, debug=False,
                   detect_race_conditions=False)
    taps = set(debug_taps)

    xTt = nc.dram_tensor("xTt", [NCH, NCI, 128, CH], F32, kind="ExternalInput").ap()
    xresi = nc.dram_tensor("xresi", [2, NCI, 128, CH], F32, kind="ExternalInput").ap()
    wqkvT = nc.dram_tensor("wqkvT", [NCI, 128, QKVO], F32, kind="ExternalInput").ap()
    wpT = nc.dram_tensor("wpT", [NCI, 128, D], F32, kind="ExternalInput").ap()
    gwT = nc.dram_tensor("gwT", [16, NCI, 128, 512], F32, kind="ExternalInput").ap()
    vwT = nc.dram_tensor("vwT", [16, NCI, 128, 512], F32, kind="ExternalInput").ap()
    owTt = nc.dram_tensor("owTt", [NHI, 128, D], F32, kind="ExternalInput").ap()
    gpg = nc.dram_tensor("gpg", [2, NCI, 128, 512], F32, kind="ExternalInput").ap()
    gpv = nc.dram_tensor("gpv", [2, NCI, 128, 512], F32, kind="ExternalInput").ap()
    gpo = nc.dram_tensor("gpo", [8, 128, D], F32, kind="ExternalInput").ap()
    gb_t = nc.dram_tensor("gb_t", [128, NHI], F32, kind="ExternalInput").ap()
    vb_t = nc.dram_tensor("vb_t", [128, NHI], F32, kind="ExternalInput").ap()
    ob_t = nc.dram_tensor("ob_t", [128, NCI], F32, kind="ExternalInput").ap()
    out = nc.dram_tensor("out", [2, CH, D], F32, kind="ExternalOutput").ap()

    ag_in = [nc.dram_tensor(f"ag_in{h}", [128, T], BF16, kind="Internal") for h in range(NHC)]
    ag_out = [nc.dram_tensor(f"ag_out{h}", [8 * 128, T], BF16, kind="Internal") for h in range(NHC)]
    gp_in = nc.dram_tensor("gp_in", [1, 4], F32, kind="Internal")
    gp_out = nc.dram_tensor("gp_out", [8, 4], F32, kind="Internal")
    res1_d = nc.dram_tensor("res1_d", [2, NCI, 128, CH], F32, kind="Internal").ap()
    gv_d = nc.dram_tensor("gv_d", [NHI, 128, 1024], FP16, kind="Internal").ap()
    ow8_d = nc.dram_tensor("ow8_d", [NHI, 128, D], FP8, kind="Internal").ap()

    masks_t = nc.inline_tensor(_causal_masks().reshape(4 * 128, CH).astype(bfdt), name="masksc")
    identf_t = nc.inline_tensor(np.eye(128, dtype=np.float32), name="identfc")
    onesc_t = nc.inline_tensor(np.ones((128, 1), np.float32).astype(bfdt), name="onescc")
    onesr_t = nc.inline_tensor(np.ones((1, 128), np.float32).astype(bfdt), name="onesrc")

    dbg = {}
    if "qkv" in taps:
        dbg["q"] = nc.dram_tensor("dbg_q", [128, NHC, T], BF16, kind="ExternalOutput").ap()
        dbg["k"] = nc.dram_tensor("dbg_k", [128, NHC, T], BF16, kind="ExternalOutput").ap()
        dbg["v"] = nc.dram_tensor("dbg_v", [128, 32, 512], BF16, kind="ExternalOutput").ap()
        dbg["brb"] = nc.dram_tensor("dbg_brb", [128, 2, CH], BF16, kind="ExternalOutput").ap()
        dbg["colb"] = nc.dram_tensor("dbg_colb", [128, 64], F32, kind="ExternalOutput").ap()
    if "y" in taps:
        dbg["y"] = nc.dram_tensor("dbg_y", [128, NHC, T], BF16, kind="ExternalOutput").ap()
    if "rows" in taps:
        dbg["rows"] = nc.dram_tensor("dbg_rows", [1, 4096], F32, kind="ExternalOutput").ap()
        dbg["rows16"] = nc.dram_tensor("dbg_rows16", [1, 1024], BF16, kind="ExternalOutput").ap()
        dbg["pst"] = nc.dram_tensor("dbg_pst", [1, 2 * CH], F32, kind="ExternalOutput").ap()
    if "res1" in taps:
        dbg["res1"] = nc.dram_tensor("dbg_res1", [2, NCI, 128, CH], F32, kind="ExternalOutput").ap()
    if "xq" in taps:
        dbg["xq"] = nc.dram_tensor("dbg_xq", [128, NCI, 1024], BF16, kind="ExternalOutput").ap()
        dbg["ab"] = nc.dram_tensor("dbg_ab", [1, 4 * CH], F32, kind="ExternalOutput").ap()
    if "gv" in taps:
        dbg["gv"] = nc.dram_tensor("dbg_gv", [NHI, 128, 1024], FP16, kind="ExternalOutput").ap()
        dbg["tern"] = nc.dram_tensor("dbg_tern", [2, 128, 2, NCI, 256], BF16, kind="ExternalOutput").ap()
        dbg["gam"] = nc.dram_tensor("dbg_gam", [1, 80], F32, kind="ExternalOutput").ap()

    P = Prog(nc)

    with ExitStack() as top:
        et = top.enter_context
        identf = et(nc.sbuf_tensor("identf", [128, 128], F32))
        onesc = et(nc.sbuf_tensor("onesc", [128, 1], BF16))
        onesr = et(nc.sbuf_tensor("onesr", [1, 128], BF16))
        gbc = et(nc.sbuf_tensor("gbc", [128, NHI], F32))
        vbc = et(nc.sbuf_tensor("vbc", [128, NHI], F32))
        obc = et(nc.sbuf_tensor("obc", [128, NCI], F32))
        rp = et(nc.sbuf_tensor("rp", [128, 4096], F32))
        rp16 = et(nc.sbuf_tensor("rp16", [128, 1024], BF16))
        colb = et(nc.sbuf_tensor("colb", [128, 64], F32))
        cmagic = et(nc.sbuf_tensor("cmagic", [128, 1], F32))
        ceps = et(nc.sbuf_tensor("ceps", [128, 1], F32))

        for s in ("BAR", "sAUX", "sSTG", "sSM", "sSMt", "sRW",
                  "sBC", "sQK", "sEV", "sS", "sP", "sPM", "sAV", "sYE", "sG",
                  "sAGD", "sAGS", "sGP", "sWP", "sPJ", "sL2",
                  "sL2T", "sQZ", "sFT", "sFM", "sFE",
                  "sO8", "sOE", "sST", "sP2", "sV2",
                  "dST0", "dST1", "dGT0", "dGT1", "dWP0", "dWP1", "dRC",
                  "dXR0", "dXR1", "dR10", "dR11", "dQR0", "dQR1",
                  "dRB0", "dRB1", "dGW0", "dGW1", "dGW2", "dGW3",
                  "dOW0", "dOW1", "dGV0", "dGV1", "dO8A", "dG20", "dG21",
                  "dOB0", "dOB1"):
            P.sem(top, s)

        def row(a, n=CH):
            return rp[0:1, a:a + n]

        def row16(a, n=CH):
            return rp16[0:1, a:a + n]

        P.op("sync", lambda s: s.dma_start(out=identf[:], in_=identf_t.ap()), inc="sAUX", inc_k=16)
        P.op("sync", lambda s: s.dma_start(out=onesc[:], in_=onesc_t.ap()), inc="sAUX", inc_k=16)
        P.op("sync", lambda s: s.dma_start(out=onesr[:], in_=onesr_t.ap()), inc="sAUX", inc_k=16)
        P.op("sync", lambda s: s.dma_start(out=gbc[:], in_=gb_t[:]), inc="sAUX", inc_k=16)
        P.op("sync", lambda s: s.dma_start(out=vbc[:], in_=vb_t[:]), inc="sAUX", inc_k=16)
        P.op("sync", lambda s: s.dma_start(out=obc[:], in_=ob_t[:]), inc="sAUX", inc_k=16)
        P.op("gpsimd", lambda g: g.load_library(_ATTN_LIB))
        P.op("scalar", lambda s: s.preload_act_table(AF.Sqrt))
        P.op("vector", lambda v: v.memset(colb[:], 0.0))
        P.op("vector", lambda v: v.memset(rp[:], 0.0))
        P.op("vector", lambda v: v.memset(cmagic[:], MAGIC))
        P.op("vector", lambda v: v.memset(ceps[:], EPS), inc="sAUX", inc_k=16)
        aux_all = P.n("sAUX")

        with ExitStack() as SATT:
            es = SATT.enter_context
            qb = es(nc.sbuf_tensor("qb", [128, NHC, T], BF16))
            kb = es(nc.sbuf_tensor("kb", [128, NHC, T], BF16))
            vbuf = es(nc.sbuf_tensor("vbuf", [128, 32, 512], BF16))

            with ExitStack() as A:
                ea = A.enter_context
                wq = ea(nc.sbuf_tensor("wq", [128, NCI, QKVO], BF16))
                xbuf = ea(nc.sbuf_tensor("xbuf", [128, NCI, CH], BF16))
                stage = ea(nc.sbuf_tensor("stage", [128, 2, CH], F32))
                xsq = ea(nc.sbuf_tensor("xsq", [128, 2, CH], BF16))
                brB = ea(nc.sbuf_tensor("brB", [128, 2, CH], BF16))
                svB = ea(nc.sbuf_tensor("svB", [128, 512], BF16))
                t1b = ea(nc.sbuf_tensor("t1b", [128, CH], F32))
                pstat = ea(nc.psum_tensor("pstat", [1, CH], F32))
                pstat2 = ea(nc.psum_tensor("pstat2", [1, CH], F32))
                pbc = ea(nc.psum_tensor("pbc", [128, CH], F32))
                pqk = [ea(nc.psum_tensor(f"pqk{i}", [128, CH], F32)) for i in range(4)]
                ptc = ea(nc.psum_tensor("ptc", [128, 16], F32))

                NWP = QKVO // CH
                for ci in range(NCI):
                    for pz in range(NWP):
                        kk = ci * NWP + pz
                        dsem = "dST%d" % (kk % 2)
                        P.op("sync", lambda s, ci=ci, pz=pz, kk=kk: s.dma_start(
                            out=stage[:, kk % 2, :], in_=wqkvT[ci][:, CH * pz:CH * (pz + 1)]),
                            waits=(("sSTG", kk - 1),), inc=dsem, inc_k=16)
                        P.op("scalar", lambda s, ci=ci, pz=pz, kk=kk: s.activation(
                            out=wq[:, ci, CH * pz:CH * (pz + 1)], in_=stage[:, kk % 2, :],
                            func=AF.Copy, bias=0.0, scale=1.0),
                            waits=((dsem, P.n(dsem)),), inc="sSTG")
                wq_rdy = P.n("sSTG")

                P.wait("tensor", "sAUX", aux_all)
                for pz in range(NWP):
                    for ci in range(NCI):
                        P.op("tensor", lambda t, pz=pz, ci=ci: t.matmul(
                            out=pstat[:], lhsT=onesc[:], rhs=wq[:, ci, CH * pz:CH * (pz + 1)],
                            start=(ci == 0), stop=(ci == NCI - 1)),
                            waits=(("sSTG", wq_rdy),),
                            inc=("sSM" if ci == NCI - 1 else None))
                    P.op("vector", lambda v: v.tensor_copy(out=row(R_T1), in_=pstat[:]),
                         waits=(("sSM", P.n("sSM")),), inc="sRW")
                    if pz < 2:
                        for qq in range(4):
                            P.op("tensor", lambda t, qq=qq: t.transpose(
                                out=ptc[:, qq:qq + 1],
                                in_=rp[0:1, R_T1 + 128 * qq:R_T1 + 128 * (qq + 1)],
                                identity=identf[0:1, 0:1]),
                                waits=(("sRW", P.n("sRW")),),
                                inc=("sSM" if qq == 3 else None))
                        P.op("vector", lambda v, pz=pz: v.tensor_copy(
                            out=colb[:, 4 * pz:4 * (pz + 1)], in_=ptc[:, 0:4]),
                            waits=(("sSM", P.n("sSM")),), inc="sRW")
                    else:
                        P.op("vector", lambda v: v.tensor_copy(out=row16(0), in_=row(R_T1)),
                             inc="sRW")
                        P.op("tensor", lambda t: t.matmul(
                            out=pbc[:], lhsT=onesr[:], rhs=row16(0), start=True, stop=True),
                            waits=(("sRW", P.n("sRW")),), inc="sSM")
                        P.op("scalar", lambda s: s.activation(
                            out=svB[:], in_=pbc[:], func=AF.Copy, bias=0.0, scale=1.0),
                            waits=(("sSM", P.n("sSM")),), inc="sBC")
                svb_rdy = P.n("sBC")
                bc_free = P.n("sBC")
                stat_free = P.n("sRW")
                ev_at = {i: 0 for i in range(4)}
                chunk_qk_done = 0
                chunk_ev_done = 0

                for ch in range(NCH):
                    for ci in range(NCI):
                        kk = ch * NCI + ci
                        dsem = "dST%d" % (kk % 2)
                        P.op("sync", lambda s, ch=ch, ci=ci, kk=kk: s.dma_start(
                            out=stage[:, kk % 2, :], in_=xTt[ch, ci]),
                            waits=(("sSTG", wq_rdy + kk - 1),), inc=dsem, inc_k=16)
                        P.op("scalar", lambda s, ci=ci, kk=kk: s.activation(
                            out=xbuf[:, ci, :], in_=stage[:, kk % 2, :], func=AF.Copy,
                            bias=0.0, scale=1.0),
                            waits=((dsem, P.n(dsem)), ("sQK", chunk_qk_done)))
                        P.op("scalar", lambda s, ci=ci, kk=kk: s.activation(
                            out=xsq[:, kk % 2, :], in_=stage[:, kk % 2, :], func=AF.Square,
                            bias=0.0, scale=1.0),
                            waits=(("sSMt", kk - 1),), inc="sSTG")
                        P.op("tensor", lambda t, ci=ci: t.matmul(
                            out=pstat[:], lhsT=onesc[:], rhs=xbuf[:, ci, :],
                            start=(ci == 0), stop=(ci == NCI - 1)),
                            waits=(("sSTG", P.n("sSTG")),
                                   ("sRW", stat_free if ci == 0 else 0)))
                        P.op("tensor", lambda t, ci=ci, kk=kk: t.matmul(
                            out=pstat2[:], lhsT=onesc[:], rhs=xsq[:, kk % 2, :],
                            start=(ci == 0), stop=(ci == NCI - 1)),
                            inc="sSMt")
                    stats_done = P.n("sSMt")

                    P.op("vector", lambda v: v.tensor_scalar(
                        out=row(R_MEAN), in0=pstat[:], scalar1=1.0 / D, scalar2=None,
                        op0=ALU.mult),
                        waits=(("sSMt", stats_done),))
                    P.op("vector", lambda v: v.tensor_scalar(
                        out=row(R_EX2), in0=pstat2[:], scalar1=1.0 / D, scalar2=None,
                        op0=ALU.mult), inc="sRW")
                    stat_free = P.n("sRW")
                    P.op("vector", lambda v: v.tensor_mul(out=row(R_T1), in0=row(R_MEAN),
                                                          in1=row(R_MEAN)))
                    P.op("vector", lambda v: v.tensor_sub(out=row(R_EX2), in0=row(R_EX2),
                                                          in1=row(R_T1)), inc="sRW")
                    P.op("vector", lambda v: v.tensor_scalar(
                        out=row(R_T2), in0=row(R_EX2), scalar1=EPS, scalar2=None,
                        op0=ALU.add))
                    P.op("vector", lambda v: v.reciprocal(out=row(R_T2), in_=row(R_T2)),
                         inc="sRW")
                    P.op("scalar", lambda s: s.activation(out=row(R_R), in_=row(R_T2),
                                                          func=AF.Sqrt, bias=0.0, scale=1.0),
                         waits=(("sRW", P.n("sRW")),), inc="sBC")
                    P.op("vector", lambda v: v.tensor_mul(out=row(R_MR), in0=row(R_MEAN),
                                                          in1=row(R_R)),
                         waits=(("sBC", P.n("sBC")),))
                    P.op("vector", lambda v: v.tensor_scalar(
                        out=row16(0), in0=row(R_R), scalar1=-1.0, scalar2=None, op0=ALU.mult))
                    P.op("vector", lambda v: v.tensor_copy(out=row16(512), in_=row(R_MR)),
                         inc="sRW")
                    rows_b16 = P.n("sRW")
                    if "rows" in taps and ch == 0:
                        P.wait("sync", "sRW", rows_b16)
                        P.op("sync", lambda s: s.dma_start(
                            out=dbg["rows"], in_=rp[0:1, :]), inc="sST", inc_k=16)
                        P.op("sync", lambda s: s.dma_start(
                            out=dbg["rows16"], in_=rp16[0:1, :]), inc="sST", inc_k=16)

                    P.op("tensor", lambda t: t.matmul(out=pbc[:], lhsT=onesr[:], rhs=row16(0),
                                                      start=True, stop=True),
                         waits=(("sRW", rows_b16), ("sBC", bc_free)), inc="sSM")
                    P.op("scalar", lambda s: s.activation(out=brB[:, 0, :], in_=pbc[:],
                                                          func=AF.Copy, bias=0.0, scale=1.0),
                         waits=(("sEV", chunk_ev_done), ("sSM", P.n("sSM"))), inc="sBC")
                    P.op("tensor", lambda t: t.matmul(out=pbc[:], lhsT=onesr[:], rhs=row16(512),
                                                      start=True, stop=True),
                         waits=(("sBC", P.n("sBC")),), inc="sSM")
                    P.op("scalar", lambda s: s.activation(out=brB[:, 1, :], in_=pbc[:],
                                                          func=AF.Copy, bias=0.0, scale=1.0),
                         waits=(("sSM", P.n("sSM")),), inc="sBC")
                    brb_rdy = P.n("sBC")
                    bc_free = brb_rdy
                    for tt in range(4):
                        P.op("tensor", lambda t, tt=tt: t.transpose(
                            out=ptc[:, 2 * tt:2 * tt + 1],
                            in_=rp[0:1, R_R + 128 * tt:R_R + 128 * (tt + 1)],
                            identity=identf[0:1, 0:1]),
                            waits=(("sRW", rows_b16),))
                        P.op("tensor", lambda t, tt=tt: t.transpose(
                            out=ptc[:, 2 * tt + 1:2 * tt + 2],
                            in_=rp[0:1, R_MR + 128 * tt:R_MR + 128 * (tt + 1)],
                            identity=identf[0:1, 0:1]),
                            inc=("sSM" if tt == 3 else None))
                    P.op("vector", lambda v: v.tensor_scalar(
                        out=colb[:, 24:28], in0=ptc[:, 0:8:2], scalar1=-1.0, scalar2=None,
                        op0=ALU.mult),
                        waits=(("sSM", P.n("sSM")),))
                    P.op("vector", lambda v: v.tensor_copy(out=colb[:, 16:24], in_=ptc[:, 0:8]),
                         inc="sRW")
                    cols_rdy = P.n("sRW")

                    for grp in range(2):
                        base = 512 * grp
                        dstb = qb if grp == 0 else kb
                        for ot in range(4):
                            for ci in range(NCI):
                                P.op("tensor", lambda t, ot=ot, ci=ci, base=base: t.matmul(
                                    out=pqk[ot][:],
                                    lhsT=wq[:, ci, base + 128 * ot:base + 128 * (ot + 1)],
                                    rhs=xbuf[:, ci, :],
                                    start=(ci == 0), stop=(ci == NCI - 1)),
                                    waits=(("sEV", ev_at[ot] if ci == 0 else 0),),
                                    inc=("sQK" if ci == NCI - 1 else None))
                            gdone = P.n("sQK")
                            if os.environ.get("KQRAW"):
                                P.op("vector", lambda v, ot=ot, dstb=dstb, ch=ch: v.tensor_copy(
                                    out=dstb[:, ot, CH * ch:CH * (ch + 1)], in_=pqk[ot][:]),
                                    waits=(("sQK", gdone), ("sBC", brb_rdy)), inc="sEV")
                                ev_at[ot] = P.n("sEV")
                                continue
                            P.op("vector", lambda v, ot=ot, grp=grp: v.scalar_tensor_tensor(
                                out=t1b[:], in0=brB[:, 1, :],
                                scalar=colb[:, 4 * grp + ot:4 * grp + ot + 1],
                                in1=pqk[ot][:], op0=ALU.mult, op1=ALU.subtract),
                                waits=(("sQK", gdone), ("sBC", brb_rdy)))
                            P.op("vector", lambda v, ot=ot, dstb=dstb, ch=ch: v.tensor_mul(
                                out=dstb[:, ot, CH * ch:CH * (ch + 1)], in0=t1b[:],
                                in1=brB[:, 0, :]), inc="sEV")
                            ev_at[ot] = P.n("sEV")
                    for tt in range(4):
                        for ci in range(NCI):
                            P.op("tensor", lambda t, tt=tt, ci=ci: t.matmul(
                                out=pqk[tt][:], lhsT=xbuf[:, ci, 128 * tt:128 * (tt + 1)],
                                rhs=wq[:, ci, 1024:1536],
                                start=(ci == 0), stop=(ci == NCI - 1)),
                                waits=(("sEV", ev_at[tt] if ci == 0 else 0),),
                                inc=("sQK" if ci == NCI - 1 else None))
                        vdone = P.n("sQK")
                        P.op("vector", lambda v, tt=tt: v.scalar_tensor_tensor(
                            out=t1b[:], in0=svB[:], scalar=colb[:, 17 + 2 * tt:18 + 2 * tt],
                            in1=pqk[tt][:], op0=ALU.mult, op1=ALU.subtract),
                            waits=(("sQK", vdone), ("sRW", cols_rdy), ("sBC", svb_rdy)))
                        P.op("vector", lambda v, tt=tt, ch=ch: v.tensor_scalar(
                            out=vbuf[:, 4 * ch + tt, :], in0=t1b[:],
                            scalar1=colb[:, 24 + tt:25 + tt], scalar2=None, op0=ALU.mult),
                            inc="sEV")
                        ev_at[tt] = P.n("sEV")
                    chunk_qk_done = P.n("sQK")
                    chunk_ev_done = P.n("sEV")

                qkv_done = P.n("sEV")
                if "qkv" in taps:
                    P.wait("sync", "sEV", qkv_done)
                    P.op("sync", lambda s: s.dma_start(out=dbg["brb"], in_=brB[:]), inc="sST", inc_k=16)
                    P.op("sync", lambda s: s.dma_start(out=dbg["colb"], in_=colb[:]), inc="sST", inc_k=16)
                    P.op("sync", lambda s: s.dma_start(out=dbg["q"], in_=qb[:]), inc="sST", inc_k=16)
                    P.op("sync", lambda s: s.dma_start(out=dbg["k"], in_=kb[:]), inc="sST", inc_k=16)
                    P.op("sync", lambda s: s.dma_start(out=dbg["v"], in_=vbuf[:]), inc="sST", inc_k=16)
                P.barrier()

            if _PHASES < 2:
                with nc.Block() as blk:
                    P.replay(blk)
                return nc
            with ExitStack() as B:
                eb = B.enter_context
                masks = eb(nc.sbuf_tensor("masks", [128, 4, CH], BF16))
                yh = eb(nc.sbuf_tensor("yh", [128, 2, T], BF16))
                pexp = eb(nc.sbuf_tensor("pexp", [128, 4, CH], BF16))
                qsqb = eb(nc.sbuf_tensor("qsqb", [128, T], BF16))
                gst = eb(nc.sbuf_tensor("gst", [128, 2, D], F32))
                gacc = eb(nc.sbuf_tensor("gacc", [128, 80], F32))
                gsc = eb(nc.sbuf_tensor("gsc", [128, 16], F32))
                invB = eb(nc.sbuf_tensor("invB", [128, 2, CH], F32))
                ps = [eb(nc.psum_tensor(f"ps{i}", [128, CH], F32)) for i in range(2)]
                py = [eb(nc.psum_tensor(f"py{i}", [128, CH], F32)) for i in range(2)]
                prs = [eb(nc.psum_tensor(f"prs{i}", [1, CH], F32)) for i in range(2)]
                prow = eb(nc.psum_tensor("prow", [1, CH], F32))
                pinv = eb(nc.psum_tensor("pinv", [128, CH], F32))

                P.op("scalar", lambda s: s.preload_act_table(AF.Exp))
                P.op("sync", lambda s: s.dma_start(
                    out=masks[:], in_=masks_t.ap().rearrange("(f p) t -> p f t", p=128)),
                    inc="sAUX", inc_k=16)
                masks_rdy = P.n("sAUX")

                gtiles = []
                for w in range(2):
                    src = gpg if w == 0 else gpv
                    for hcs in range(2):
                        for ci in range(NCI):
                            gtiles.append((src[hcs, ci], CH))
                for hb in range(8):
                    gtiles.append((gpo[hb], D))
                for i, (ap, fd) in enumerate(gtiles):
                    dsem = "dGT%d" % (i % 2)
                    P.op("sync", lambda s, ap=ap, i=i, fd=fd: s.dma_start(
                        out=gst[:, i % 2, 0:fd], in_=ap),
                        waits=(("sGP", 2 * (i - 1)),), inc=dsem, inc_k=16)
                    P.op("vector", lambda v, i=i, fd=fd: v.tensor_reduce(
                        out=gsc[:, 0:16].rearrange("p (a b) -> p a b", a=16),
                        in_=gst[:, i % 2, 0:fd].rearrange("p (a b) -> p a b", a=16),
                        axis=AX, op=ALU.add, apply_absolute_value=True),
                        waits=((dsem, P.n(dsem)),), inc="sGP")
                    P.wait("vector", "sGP", P.n("sGP"))
                    P.op("vector", lambda v, i=i: v.tensor_reduce(
                        out=gacc[:, i:i + 1], in_=gsc[:, 0:16], axis=AX,
                        op=ALU.add),
                        inc="sGP")
                ngt = len(gtiles)
                P.op("vector", lambda v: v.tensor_reduce(
                    out=gacc[:, 72:73], in_=gacc[:, 0:32], axis=AX, op=ALU.add),
                    waits=(("sGP", 2 * ngt),))
                P.op("vector", lambda v: v.tensor_reduce(
                    out=gacc[:, 73:74], in_=gacc[:, 32:64], axis=AX, op=ALU.add))
                P.op("vector", lambda v: v.tensor_reduce(
                    out=gacc[:, 74:75], in_=gacc[:, 64:72], axis=AX, op=ALU.add))
                P.op("vector", lambda v: v.memset(gacc[:, 75:76], 0.0), inc="sGP")
                P.op("gpsimd", lambda g: g.partition_all_reduce(
                    gacc[:, 76:80], gacc[:, 72:76], 128, ReduceOp.add),
                    waits=(("sGP", P.n("sGP")),), inc="sG")
                P.op("gpsimd", lambda g: g.dma_start(out=gp_in.ap(), in_=gacc[0:1, 76:80]),
                     inc="sAGS", inc_k=16)
                P.op("gpsimd", lambda g: g.collective_compute(
                    "AllGather", ALU.bypass, RG, ins=[gp_in.ap()], outs=[gp_out.ap()]),
                    waits=(("sAGS", P.n("sAGS")),), inc="sAGD")
                gp_agd = P.n("sAGD")
                P.op("gpsimd", lambda g: g.dma_start(
                    out=gacc[0:1, 0:32].rearrange("p (a b) -> p a b", a=4),
                    in_=gp_out.ap().transpose([1, 0]),
                    allow_slow_non_contiguous=True),
                    waits=(("sAGD", gp_agd),), inc="sAGS", inc_k=16)
                P.op("vector", lambda v: v.tensor_reduce(
                    out=gacc[0:1, 32:36].rearrange("p (a b) -> p a b", a=4),
                    in_=gacc[0:1, 0:32].rearrange("p (a b) -> p a b", a=4),
                    axis=AX, op=ALU.add),
                    waits=(("sAGS", P.n("sAGS")),), inc="sGP")
                P.wait("vector", "sGP", P.n("sGP"))
                P.op("vector", lambda v: v.tensor_scalar(
                    out=gacc[0:1, 36:40], in0=gacc[0:1, 32:36], scalar1=1.0 / NW,
                    scalar2=EPS, op0=ALU.mult, op1=ALU.max), inc="sGP")
                P.wait("vector", "sGP", P.n("sGP"))
                P.op("vector", lambda v: v.reciprocal(out=gacc[0:1, 40:44], in_=gacc[0:1, 36:40]),
                     inc="sGP")
                P.wait("vector", "sGP", P.n("sGP"))
                P.op("vector", lambda v: v.tensor_mul(out=gacc[0:1, 44:48], in0=gacc[0:1, 36:40],
                                                      in1=gacc[0:1, 40:44]), inc="sGP")
                P.wait("vector", "sGP", P.n("sGP"))
                P.op("vector", lambda v: v.tensor_scalar(
                    out=gacc[0:1, 44:48], in0=gacc[0:1, 44:48], scalar1=-1.0, scalar2=2.0,
                    op0=ALU.mult, op1=ALU.add), inc="sGP")
                P.wait("vector", "sGP", P.n("sGP"))
                P.op("vector", lambda v: v.tensor_mul(out=gacc[0:1, 40:44], in0=gacc[0:1, 40:44],
                                                      in1=gacc[0:1, 44:48]), inc="sGP")
                P.wait("vector", "sGP", P.n("sGP"))
                P.op("vector", lambda v: v.tensor_scalar(
                    out=gacc[0:1, 48:52], in0=gacc[0:1, 36:40], scalar1=0.5, scalar2=None,
                    op0=ALU.mult))
                P.op("vector", lambda v: v.tensor_scalar(
                    out=gacc[0:1, 52:56], in0=gacc[0:1, 36:40], scalar1=-0.5, scalar2=None,
                    op0=ALU.mult), inc="sGP")
                P.wait("vector", "sGP", P.n("sGP"))
                P.op("vector", lambda v: v.tensor_copy(out=rp[0:1, R_GAM:R_GAM + 8],
                                                       in_=gacc[0:1, 36:44]), inc="sGP")
                gam_rdy = P.n("sGP")
                P.op("gpsimd", lambda g: g.partition_broadcast(colb[:, 32:36], gacc[0:1, 40:44], 128),
                     waits=(("sGP", gam_rdy),), inc="sG")
                P.op("gpsimd", lambda g: g.partition_broadcast(colb[:, 40:44], gacc[0:1, 48:52], 128))
                P.op("gpsimd", lambda g: g.partition_broadcast(colb[:, 44:48], gacc[0:1, 52:56], 128),
                     inc="sG")
                invg_rdy = P.n("sG")
                if "gv" in taps:
                    P.wait("sync", "sGP", gam_rdy)
                    P.op("sync", lambda s: s.dma_start(out=dbg["gam"], in_=gacc[0:1, 0:80]),
                         inc="sST", inc_k=16)

                yev_at = [0, 0]
                rsev_at = [0, 0]
                agdma_at = [0, 0]
                invev_at = [0, 0]
                exp_hist = []
                av_hist = []
                negb_at = [0, 0]
                exp_at_head = [0, 0, 0, 0]

                def emit_bound(hh):
                    for which, buf in ((0, qb), (1, kb)):
                        P.op("scalar", lambda s, buf=buf, hh=hh: s.activation(
                            out=qsqb[:], in_=buf[:, hh, :], func=AF.Square, bias=0.0,
                            scale=1.0),
                            waits=(("sS", P.n("sS")),), inc="sPM")
                        sq_done = P.n("sPM")
                        for j in range(NCH):
                            P.op("tensor", lambda t, j=j: t.matmul(
                                out=prow[:], lhsT=onesc[:], rhs=qsqb[:, CH * j:CH * (j + 1)],
                                start=True, stop=True),
                                waits=(("sPM", sq_done), ("sRW", P.n("sRW"))),
                                inc="sS")
                            P.op("vector", lambda v, j=j: v.tensor_reduce(
                                out=rp[0:1, R_SCA + j:R_SCA + j + 1], in_=prow[:], axis=AX,
                                op=ALU.max),
                                waits=(("sS", P.n("sS")),), inc="sRW")
                        P.op("vector", lambda v, which=which: v.tensor_reduce(
                            out=rp[0:1, R_SCA + 8 + which:R_SCA + 9 + which],
                            in_=rp[0:1, R_SCA:R_SCA + 8], axis=AX, op=ALU.max),
                            inc="sRW")
                    P.op("vector", lambda v: v.tensor_add(
                        out=rp[0:1, R_SCA + 10:R_SCA + 11], in0=rp[0:1, R_SCA + 8:R_SCA + 9],
                        in1=rp[0:1, R_SCA + 9:R_SCA + 10]))
                    P.op("vector", lambda v: v.tensor_scalar(
                        out=rp[0:1, R_SCA + 11:R_SCA + 12], in0=rp[0:1, R_SCA + 10:R_SCA + 11],
                        scalar1=-0.5 * ISD, scalar2=None, op0=ALU.mult),
                        waits=(("sG", negb_at[(hh - 1) % 2]),), inc="sRW")
                    P.op("gpsimd", lambda g, hh=hh: g.partition_broadcast(
                        colb[:, 36 + hh % 2:37 + hh % 2], rp[0:1, R_SCA + 11:R_SCA + 12], 128),
                        waits=(("sRW", P.n("sRW")),
                               ("sP", exp_at_head[hh - 2] if hh >= 2 else 0)),
                        inc="sG")
                    negb_at[hh % 2] = P.n("sG")

                emit_bound(0)
                for h in range(NHC):
                    negb_rdy = negb_at[h % 2]
                    for j in range(NCH):
                        ntk = 4 * (j + 1)
                        pysel = py[j % 2]
                        prssel = prs[j % 2]
                        for tk in range(ntk):
                            gi = len(exp_hist)
                            P.op("tensor", lambda t, h=h, j=j, tk=tk: t.matmul(
                                out=ps[tk % 2][:],
                                lhsT=kb[:, h, 128 * tk:128 * (tk + 1)],
                                rhs=qb[:, h, CH * j:CH * (j + 1)], start=True, stop=True),
                                waits=(("sP", exp_hist[gi - 2] if gi >= 2 else 0),
                                       ("sYE", yev_at[j % 2] if tk == 0 else 0),
                                       ("sRW", rsev_at[j % 2] if tk == 0 else 0)),
                                inc="sS")
                            P.op("scalar", lambda s, tk=tk, h=h: s.activation(
                                out=pexp[:, tk % 4, :], in_=ps[tk % 2][:],
                                func=AF.Exp, bias=colb[:, 36 + h % 2:37 + h % 2], scale=ISD),
                                waits=(("sS", P.n("sS")), ("sG", negb_rdy),
                                       ("sAV", av_hist[gi - 4] if gi >= 4 else 0),
                                       ("sAUX", masks_rdy)),
                                inc="sP")
                            if tk >= ntk - 4:
                                rel = tk - (ntk - 4)
                                P.op("vector", lambda v, tk=tk, rel=rel: v.tensor_mul(
                                    out=pexp[:, tk % 4, :], in0=pexp[:, tk % 4, :],
                                    in1=masks[:, rel, :]),
                                    waits=(("sP", P.n("sP")),), inc="sP")
                            exp_hist.append(P.n("sP"))
                            P.op("tensor", lambda t, tk=tk, ntk=ntk, prssel=prssel: t.matmul(
                                out=prssel[:], lhsT=onesc[:], rhs=pexp[:, tk % 4, :],
                                start=(tk == 0), stop=(tk == ntk - 1)),
                                waits=(("sP", exp_hist[-1]),))
                            P.op("tensor", lambda t, h=h, tk=tk, ntk=ntk, pysel=pysel: t.matmul(
                                out=pysel[:], lhsT=vbuf[:, tk, 128 * h:128 * (h + 1)],
                                rhs=pexp[:, tk % 4, :],
                                start=(tk == 0), stop=(tk == ntk - 1)),
                                inc="sAV")
                            av_hist.append(P.n("sAV"))
                        av_done = P.n("sAV")
                        P.op("vector", lambda v, prssel=prssel: v.tensor_copy(
                            out=row(R_T1), in_=prssel[:]),
                            waits=(("sAV", av_done),), inc="sRW")
                        rsev_at[j % 2] = P.n("sRW")
                        P.op("vector", lambda v: v.reciprocal(out=row(R_T2), in_=row(R_T1)))
                        P.op("vector", lambda v: v.tensor_copy(out=row16(0), in_=row(R_T2)),
                             waits=(("sSM", P.n("sSM")),), inc="sRW")
                        P.op("tensor", lambda t: t.matmul(
                            out=pinv[:], lhsT=onesr[:], rhs=row16(0), start=True, stop=True),
                            waits=(("sRW", P.n("sRW")),), inc="sSM")
                        P.op("scalar", lambda s, j=j: s.activation(
                            out=invB[:, j % 2, :], in_=pinv[:], func=AF.Copy, bias=0.0,
                            scale=1.0),
                            waits=(("sSM", P.n("sSM")), ("sYE", yev_at[j % 2])),
                            inc="sBC")
                        P.op("vector", lambda v, h=h, j=j, pysel=pysel: v.scalar_tensor_tensor(
                            out=yh[:, h % 2, CH * j:CH * (j + 1)], in0=pysel[:], scalar=1.0,
                            in1=invB[:, j % 2, :], op0=ALU.mult, op1=ALU.mult),
                            waits=(("sBC", P.n("sBC")), ("sAGS", agdma_at[h % 2])),
                            inc="sYE")
                        yev_at[j % 2] = P.n("sYE")
                    exp_at_head[h] = P.n("sP")
                    if h + 1 < NHC:
                        emit_bound(h + 1)
                    P.op("gpsimd", lambda g, h=h: g.dma_start(
                        out=ag_in[h].ap(), in_=yh[:, h % 2, :]),
                        waits=(("sYE", P.n("sYE")),), inc="sAGS", inc_k=16)
                    agdma_at[h % 2] = P.n("sAGS")
                    P.op("gpsimd", lambda g, h=h: g.collective_compute(
                        "AllGather", ALU.bypass, RG, ins=[ag_in[h].ap()], outs=[ag_out[h].ap()]),
                        waits=(("sAGS", P.n("sAGS")),), inc="sAGD")
                    if "y" in taps:
                        P.op("sync", lambda s, h=h: s.dma_start(
                            out=dbg["y"][:, h, :], in_=yh[:, h % 2, :]),
                            waits=(("sYE", P.n("sYE")),), inc="sST", inc_k=16)
                P.barrier()

        if _PHASES < 3:
            with nc.Block() as blk:
                P.replay(blk)
            return nc
        with ExitStack() as S34:
            e34 = S34.enter_context
            xq = e34(nc.sbuf_tensor("xq", [128, NCI, 1024], BF16))
            gxB = e34(nc.sbuf_tensor("gxB", [128, 4, CH], F32))
            scGV = e34(nc.sbuf_tensor("scGV", [128, 4, CH], F32))

            with ExitStack() as Cx:
                ec = Cx.enter_context
                wpb = ec(nc.sbuf_tensor("wpb", [128, NCI, D], BF16))
                recv = ec(nc.sbuf_tensor("recv", [128, 32, CH], BF16))
                wpst = ec(nc.sbuf_tensor("wpst", [128, 2, 1024], F32))
                xrst = ec(nc.sbuf_tensor("xrst", [128, 2, CH], F32))
                r1t = ec(nc.sbuf_tensor("r1t", [128, 2, CH], F32))
                s16 = ec(nc.sbuf_tensor("s16", [128, 2, CH], BF16))
                r116 = ec(nc.sbuf_tensor("r116", [128, 2, CH], BF16))
                mmx = ec(nc.sbuf_tensor("mmx", [128, 2, CH], F32))
                mmn = ec(nc.sbuf_tensor("mmn", [128, 2, CH], F32))
                qt1 = ec(nc.sbuf_tensor("qt1", [128, CH], F32))
                qt2 = ec(nc.sbuf_tensor("qt2", [128, CH], FP16))
                qrd = ec(nc.sbuf_tensor("qrd", [128, 2, CH], F32))
                ppj = [ec(nc.psum_tensor(f"ppj{i}", [128, CH], F32)) for i in range(4)]
                pl1 = ec(nc.psum_tensor("pl1", [1, CH], F32))
                pl2 = ec(nc.psum_tensor("pl2", [1, CH], F32))

                P.op("scalar", lambda s: s.preload_act_table(AF.Sqrt))
                for ci in range(NCI):
                    for hz in range(2):
                        kk = 2 * ci + hz
                        dsem = "dWP%d" % (kk % 2)
                        P.op("sync", lambda s, ci=ci, hz=hz, kk=kk: s.dma_start(
                            out=wpst[:, kk % 2, :], in_=wpT[ci][:, 1024 * hz:1024 * (hz + 1)]),
                            waits=(("sWP", kk - 1),), inc=dsem, inc_k=16)
                        P.op("scalar", lambda s, ci=ci, hz=hz, kk=kk: s.activation(
                            out=wpb[:, ci, 1024 * hz:1024 * (hz + 1)], in_=wpst[:, kk % 2, :],
                            func=AF.Copy, bias=0.0, scale=1.0),
                            waits=((dsem, P.n(dsem)),), inc="sWP")
                wp_rdy = P.n("sWP")

                for h in range(NHC):
                    P.op("gpsimd", lambda g, h=h: g.dma_start(
                        out=recv[:, h:32:4, :].rearrange("p r t -> p r t"),
                        in_=ag_out[h].ap()[:, bass.ts(nc.partition_id(), CH)]
                        .rearrange("(r p) t -> p r t", p=128)),
                        waits=(("sAGD", gp_agd + h + 1),), inc="dRC", inc_k=16)
                recv_all = P.n("dRC")

                ridx = 0
                xr_at = {}
                ppj_free = {o: 0 for o in range(4)}
                r1sp_at = {}
                l2t_hist = []
                for bb in range(2):
                    for oti in range(NCI):
                        ri = NCI * bb + oti
                        dsem = "dXR%d" % (ri % 2)
                        P.op("sync", lambda s, bb=bb, oti=oti, ri=ri: s.dma_start(
                            out=xrst[:, ri % 2, :], in_=xresi[bb, oti]),
                            waits=(("sPJ", ri - 1),), inc=dsem, inc_k=16)
                        xr_at[(bb, oti)] = (dsem, P.n(dsem))
                    for og in range(4):
                        for ci in range(NCI):
                            for o in range(4):
                                oti = 4 * og + o
                                P.op("tensor", lambda t, bb=bb, ci=ci, o=o, oti=oti: t.matmul(
                                    out=ppj[o][:], lhsT=wpb[:, ci, 128 * oti:128 * (oti + 1)],
                                    rhs=recv[:, 16 * bb + ci, :],
                                    start=(ci == 0), stop=(ci == NCI - 1)),
                                    waits=(("sWP", wp_rdy), ("dRC", recv_all),
                                           ("sPJ", ppj_free[o] if ci == 0 else 0)),
                                    inc=("sQK" if ci == NCI - 1 else None))
                        grp_done = P.n("sQK")
                        for o in range(4):
                            oti = 4 * og + o
                            P.op("vector", lambda v, o=o, ridx=ridx: v.tensor_add(
                                out=r1t[:, ridx % 2, :], in0=ppj[o][:], in1=xrst[:, ridx % 2, :]),
                                waits=(("sQK", grp_done - 3 + o), xr_at[(bb, oti)],
                                       ("dR1%d" % (ridx % 2), r1sp_at.get(ridx - 2, 0))),
                                inc="sPJ")
                            ppj_free[o] = P.n("sPJ")
                            r1_rdy = P.n("sPJ")
                            P.op("scalar", lambda s, ridx=ridx: s.activation(
                                out=r116[:, ridx % 2, :], in_=r1t[:, ridx % 2, :],
                                func=AF.Copy, bias=0.0, scale=1.0),
                                waits=(("sPJ", r1_rdy),
                                       ("sL2T", l2t_hist[ridx - 2] if ridx >= 2 else 0)))
                            P.op("scalar", lambda s, ridx=ridx: s.activation(
                                out=s16[:, ridx % 2, :], in_=r1t[:, ridx % 2, :],
                                func=AF.Square, bias=0.0, scale=1.0),
                                inc="sL2")
                            st_rdy = P.n("sL2")
                            first = (og == 0 and o == 0)
                            last = (og == 3 and o == 3)
                            P.op("tensor", lambda t, ridx=ridx, first=first, last=last: t.matmul(
                                out=pl1[:], lhsT=onesc[:], rhs=r116[:, ridx % 2, :],
                                start=first, stop=last),
                                waits=(("sL2", st_rdy),
                                       ("sRW", stat_free if first else 0)))
                            P.op("tensor", lambda t, ridx=ridx, first=first, last=last: t.matmul(
                                out=pl2[:], lhsT=onesc[:], rhs=s16[:, ridx % 2, :],
                                start=first, stop=last),
                                inc="sL2T")
                            l2t_hist.append(P.n("sL2T"))
                            if first:
                                P.op("vector", lambda v, bb=bb, ridx=ridx: v.tensor_copy(
                                    out=mmx[:, bb, :], in_=r1t[:, ridx % 2, :]),
                                    waits=(("sPJ", r1_rdy),))
                                P.op("vector", lambda v, bb=bb, ridx=ridx: v.tensor_copy(
                                    out=mmn[:, bb, :], in_=r1t[:, ridx % 2, :]), inc="sQZ")
                            else:
                                P.op("vector", lambda v, bb=bb, ridx=ridx: v.tensor_tensor(
                                    out=mmx[:, bb, :], in0=mmx[:, bb, :],
                                    in1=r1t[:, ridx % 2, :], op=ALU.max),
                                    waits=(("sPJ", r1_rdy),))
                                P.op("vector", lambda v, bb=bb, ridx=ridx: v.tensor_tensor(
                                    out=mmn[:, bb, :], in0=mmn[:, bb, :],
                                    in1=r1t[:, ridx % 2, :], op=ALU.min), inc="sQZ")
                            dsem = "dR1%d" % (ridx % 2)
                            P.op("gpsimd", lambda g, bb=bb, oti=oti, ridx=ridx: g.dma_start(
                                out=res1_d[bb, oti], in_=r1t[:, ridx % 2, :]),
                                waits=(("sQZ", P.n("sQZ")),), inc=dsem, inc_k=16)
                            r1sp_at[ridx] = P.n(dsem)
                            ridx += 1
                    l2_done = P.n("sL2T")
                    mm_done = P.n("sQZ")
                    P.op("vector", lambda v, bb=bb: v.tensor_scalar(
                        out=mmn[:, bb, :], in0=mmn[:, bb, :], scalar1=-1.0, scalar2=None,
                        op0=ALU.mult),
                        waits=(("sQZ", mm_done),), inc="sQZ")
                    P.op("gpsimd", lambda g, bb=bb: g.partition_all_reduce(
                        mmx[:, bb, :], mmx[:, bb, :], 128, ReduceOp.max),
                        waits=(("sQZ", P.n("sQZ")),))
                    P.op("gpsimd", lambda g, bb=bb: g.partition_all_reduce(
                        mmn[:, bb, :], mmn[:, bb, :], 128, ReduceOp.max),
                        inc="sG")
                    par_done = P.n("sG")
                    P.op("vector", lambda v: v.tensor_scalar(
                        out=row(R_MEAN), in0=pl1[:], scalar1=1.0 / D, scalar2=None,
                        op0=ALU.mult),
                        waits=(("sL2T", l2_done), ("sG", par_done)))
                    P.op("vector", lambda v: v.tensor_scalar(
                        out=row(R_EX2), in0=pl2[:], scalar1=1.0 / D, scalar2=None,
                        op0=ALU.mult), inc="sRW")
                    stat_free = P.n("sRW")
                    P.op("vector", lambda v: v.tensor_mul(out=row(R_T1), in0=row(R_MEAN),
                                                          in1=row(R_MEAN)))
                    P.op("vector", lambda v: v.tensor_sub(out=row(R_EX2), in0=row(R_EX2),
                                                          in1=row(R_T1)), inc="sRW")
                    P.op("vector", lambda v: v.tensor_scalar(
                        out=row(R_T1), in0=row(R_EX2), scalar1=EPS, scalar2=None,
                        op0=ALU.add))
                    P.op("vector", lambda v: v.reciprocal(out=row(R_T1), in_=row(R_T1)),
                         inc="sRW")
                    P.op("scalar", lambda s: s.activation(out=row(R_R), in_=row(R_T1),
                                                          func=AF.Sqrt, bias=0.0, scale=1.0),
                         waits=(("sRW", P.n("sRW")),), inc="sBC")
                    P.op("vector", lambda v, bb=bb: v.tensor_sub(
                        out=row(R_T1), in0=mmx[0:1, bb, :], in1=row(R_MEAN)),
                        waits=(("sBC", P.n("sBC")),))
                    P.op("vector", lambda v, bb=bb: v.tensor_add(
                        out=row(R_T2), in0=row(R_MEAN), in1=mmn[0:1, bb, :]))
                    P.op("vector", lambda v: v.tensor_tensor(
                        out=row(R_T1), in0=row(R_T1), in1=row(R_T2), op=ALU.max))
                    P.op("vector", lambda v: v.tensor_mul(out=row(R_T2), in0=row(R_T1),
                                                          in1=row(R_R)))
                    P.op("vector", lambda v: v.tensor_scalar(
                        out=row(R_T2), in0=row(R_T2), scalar1=EPS, scalar2=None, op0=ALU.max))
                    P.op("vector", lambda v: v.reciprocal(out=row(R_T3), in_=row(R_T2)))
                    P.op("vector", lambda v: v.tensor_mul(out=row(R_EX2), in0=row(R_T2),
                                                          in1=row(R_T3)))
                    P.op("vector", lambda v: v.tensor_scalar(
                        out=row(R_EX2), in0=row(R_EX2), scalar1=-1.0, scalar2=2.0,
                        op0=ALU.mult, op1=ALU.add))
                    P.op("vector", lambda v: v.tensor_mul(out=row(R_T3), in0=row(R_T3),
                                                          in1=row(R_EX2)))
                    P.op("vector", lambda v: v.tensor_mul(out=row(R_T1), in0=row(R_R),
                                                          in1=row(R_T3)))
                    P.op("vector", lambda v: v.tensor_scalar(
                        out=row(R_T1), in0=row(R_T1), scalar1=127.0, scalar2=None,
                        op0=ALU.mult))
                    P.op("vector", lambda v: v.tensor_mul(out=row(R_T3), in0=row(R_MEAN),
                                                          in1=row(R_T1)))
                    P.op("vector", lambda v: v.tensor_scalar(
                        out=row(R_T3), in0=row(R_T3), scalar1=-1.0, scalar2=MAGIC,
                        op0=ALU.mult, op1=ALU.add))
                    P.op("vector", lambda v: v.tensor_scalar(
                        out=row(R_MR), in0=row(R_T2),
                        scalar1=rp[0:1, R_GAM + 0:R_GAM + 1], scalar2=1.0 / 127.0,
                        op0=ALU.mult, op1=ALU.mult),
                        waits=(("sGP", gam_rdy),))
                    P.op("vector", lambda v: v.tensor_scalar(
                        out=row(R_T2), in0=row(R_T2),
                        scalar1=rp[0:1, R_GAM + 1:R_GAM + 2], scalar2=1.0 / 127.0,
                        op0=ALU.mult, op1=ALU.mult), inc="sRW")
                    qrows_rdy = P.n("sRW")
                    P.op("gpsimd", lambda g, bb=bb: g.partition_broadcast(
                        gxB[:, 2 * bb + 0, :], rp[0:1, R_T1:R_T1 + CH], 128),
                        waits=(("sRW", qrows_rdy),))
                    P.op("gpsimd", lambda g, bb=bb: g.partition_broadcast(
                        gxB[:, 2 * bb + 1, :], rp[0:1, R_T3:R_T3 + CH], 128))
                    P.op("gpsimd", lambda g, bb=bb: g.partition_broadcast(
                        scGV[:, 0 + bb, :], rp[0:1, R_MR:R_MR + CH], 128))
                    P.op("gpsimd", lambda g, bb=bb: g.partition_broadcast(
                        scGV[:, 2 + bb, :], rp[0:1, R_T2:R_T2 + CH], 128), inc="sG")
                    qb_rdy = P.n("sG")
                    if "xq" in taps and bb == 0:
                        P.wait("sync", "sRW", qrows_rdy)
                        P.op("sync", lambda s: s.dma_start(
                            out=dbg["ab"][0:1, 0:CH], in_=row(R_T1)), inc="sST", inc_k=16)
                        P.op("sync", lambda s: s.dma_start(
                            out=dbg["ab"][0:1, CH:2 * CH], in_=row(R_T3)), inc="sST", inc_k=16)

                    qz_hist = {}
                    for ci in range(NCI):
                        dsem = "dQR%d" % (ci % 2)
                        P.op("sync", lambda s, bb=bb, ci=ci: s.dma_start(
                            out=qrd[:, ci % 2, :], in_=res1_d[bb, ci]),
                            waits=(("dR10", P.n("dR10")), ("dR11", P.n("dR11")),
                                   ("sQZ", qz_hist.get(ci - 2, 0))),
                            inc=dsem, inc_k=16)
                        xr_now = (dsem, P.n(dsem))
                        P.op("vector", lambda v, ci=ci, bb=bb: v.scalar_tensor_tensor(
                            out=qt1[:], in0=qrd[:, ci % 2, :], scalar=1.0,
                            in1=gxB[:, 2 * bb + 0, :], op0=ALU.mult, op1=ALU.mult),
                            waits=(xr_now, ("sG", qb_rdy)))
                        P.op("vector", lambda v, ci=ci, bb=bb: v.tensor_add(
                            out=qt2[:], in0=qt1[:], in1=gxB[:, 2 * bb + 1, :]))
                        P.op("vector", lambda v: v.tensor_scalar(
                            out=qt2[:], in0=qt2[:], scalar1=MAGIC - 128.0,
                            scalar2=MAGIC + 127.0, op0=ALU.max, op1=ALU.min))
                        P.op("vector", lambda v, ci=ci, bb=bb: v.tensor_scalar(
                            out=xq[:, ci, CH * bb:CH * (bb + 1)], in0=qt2[:],
                            scalar1=-MAGIC, scalar2=None, op0=ALU.add),
                            inc="sQZ")
                        qz_hist[ci] = P.n("sQZ")
                if "res1" in taps:
                    P.wait("sync", "dR10", P.n("dR10"))
                    P.wait("sync", "dR11", P.n("dR11"))
                    P.op("sync", lambda s: s.dma_start(out=dbg["res1"], in_=res1_d), inc="sST", inc_k=16)
                if "xq" in taps:
                    P.wait("sync", "sQZ", P.n("sQZ"))
                    P.op("sync", lambda s: s.dma_start(out=dbg["xq"], in_=xq[:]), inc="sST", inc_k=16)
                P.barrier()

            if _PHASES < 4:
                with nc.Block() as blk:
                    P.replay(blk)
                return nc
            with ExitStack() as Dx:
                ed = Dx.enter_context
                terng = ed(nc.sbuf_tensor("terng", [128, 2, NCI, 256], BF16))
                ternv = ed(nc.sbuf_tensor("ternv", [128, 2, NCI, 256], BF16))
                gwst = ed(nc.sbuf_tensor("gwst", [128, 4, 256], F32))
                tgf = ed(nc.sbuf_tensor("tgf", [128, 2, 256], FP16))
                tgf2 = ed(nc.sbuf_tensor("tgf2", [128, 2, 256], FP16))
                owst = ed(nc.sbuf_tensor("owst", [128, 2, 1024], F32))
                owf = ed(nc.sbuf_tensor("owf", [128, 1024], FP16))
                owf2 = ed(nc.sbuf_tensor("owf2", [128, 1024], FP16))
                ow8b = ed(nc.sbuf_tensor("ow8b", [128, 1024], FP8))
                e1b = ed(nc.sbuf_tensor("e1b", [128, CH], F32))
                e2b = ed(nc.sbuf_tensor("e2b", [128, CH], F32))
                g16 = ed(nc.sbuf_tensor("g16", [128, CH], F32))
                v16 = ed(nc.sbuf_tensor("v16", [128, CH], F32))
                gvst = ed(nc.sbuf_tensor("gvst", [128, 2, CH], FP16))
                mmgx = ed(nc.sbuf_tensor("mmgx", [128, 2, CH], F32))
                mmgn = ed(nc.sbuf_tensor("mmgn", [128, 2, CH], F32))
                pgv = [ed(nc.psum_tensor(f"pgv{i}", [128, CH], F32)) for i in range(8)]

                P.op("scalar", lambda s: s.preload_act_table(AF.Silu))
                P.op("vector", lambda v: v.memset(mmgx[:], 0.0))
                P.op("vector", lambda v: v.memset(mmgn[:], 0.0), inc="sQZ")
                mmgx_init = P.n("sQZ")

                HC2 = 32
                tern_rdy = {}
                mm_at = {}
                gdma = 0
                ow_i = 0
                ow_dma_at = {}
                ow8_spill_at = {}
                gv_spill = 0
                gvsp_at = {}
                pgv_free = {i: 0 for i in range(8)}
                for hc2 in range(HC2):
                    hcs, half = hc2 // 2, hc2 % 2
                    par = hc2 % 2
                    for wsel in range(2):
                        srcw = gwT if wsel == 0 else vwT
                        dstt = terng if wsel == 0 else ternv
                        for ci in range(NCI):
                            gdma += 1
                            dsem = "dGW%d" % ((gdma - 1) % 4)
                            P.op("sync", lambda s, hcs=hcs, half=half, ci=ci, wsel=wsel,
                                 gdma=gdma, srcw=srcw: s.dma_start(
                                     out=gwst[:, (gdma - 1) % 4, :],
                                     in_=srcw[hcs, ci][:, 256 * half:256 * (half + 1)]),
                                waits=(("sFT", gdma - 4),), inc=dsem, inc_k=16)
                            P.op("vector", lambda v, gdma=gdma, wsel=wsel: v.tensor_scalar(
                                out=tgf[:, (gdma - 1) % 2, :], in0=gwst[:, (gdma - 1) % 4, :],
                                scalar1=colb[:, 40 + wsel:41 + wsel], scalar2=None,
                                op0=ALU.is_gt),
                                waits=((dsem, P.n(dsem)), ("sG", invg_rdy),
                                       ("sFT", gdma - 2)), inc="sL2")
                            P.op("vector", lambda v, gdma=gdma, wsel=wsel: v.tensor_scalar(
                                out=tgf2[:, (gdma - 1) % 2, :], in0=gwst[:, (gdma - 1) % 4, :],
                                scalar1=colb[:, 44 + wsel:45 + wsel], scalar2=None,
                                op0=ALU.is_lt),
                                waits=(("sL2", P.n("sL2")),), inc="sP2")
                            P.wait("vector", "sP2", P.n("sP2"))
                            P.op("vector", lambda v, par=par, ci=ci, gdma=gdma, dstt=dstt: v.tensor_tensor(
                                out=dstt[:, par, ci, :], in0=tgf[:, (gdma - 1) % 2, :],
                                in1=tgf2[:, (gdma - 1) % 2, :], op=ALU.subtract),
                                waits=(("sFM", mm_at.get(hc2 - 2, 0)),),
                                inc="sFT")
                    tern_rdy[hc2] = P.n("sFT")
                    for ht in range(2):
                        hi = 2 * hc2 + ht
                        for wsel in range(2):
                            srct = terng if wsel == 0 else ternv
                            for ci in range(NCI):
                                for tc in range(2):
                                    bank = 4 * (hi % 2) + 2 * wsel + tc
                                    P.op("tensor", lambda t, par=par, ci=ci, ht=ht, tc=tc,
                                         bank=bank, srct=srct: t.matmul(
                                             out=pgv[bank][:],
                                             lhsT=srct[:, par, ci, 128 * ht:128 * (ht + 1)],
                                             rhs=xq[:, ci, CH * tc:CH * (tc + 1)],
                                             start=(ci == 0), stop=(ci == NCI - 1)),
                                        waits=(("sFT", tern_rdy[hc2]),
                                               ("sFE", pgv_free[bank] if ci == 0 else 0)),
                                        inc=("sFM" if ci == NCI - 1 else None))
                        mm_done = P.n("sFM")
                        mm_at[hc2] = mm_done
                        for tc in range(2):
                            bg = 4 * (hi % 2) + tc
                            bv = 4 * (hi % 2) + 2 + tc
                            P.op("vector", lambda v, bg=bg, tc=tc: v.scalar_tensor_tensor(
                                out=e1b[:], in0=pgv[bg][:], scalar=1.0,
                                in1=scGV[:, 0 + tc, :], op0=ALU.mult, op1=ALU.mult),
                                waits=(("sFM", mm_done),), inc="sQK")
                            P.op("scalar", lambda s, hi=hi: s.activation(
                                out=g16[:], in_=e1b[:], func=AF.Silu,
                                bias=gbc[:, hi:hi + 1], scale=1.0),
                                waits=(("sQK", P.n("sQK")),), inc="sBC")
                            P.op("vector", lambda v, bv=bv, tc=tc: v.scalar_tensor_tensor(
                                out=e2b[:], in0=pgv[bv][:], scalar=1.0,
                                in1=scGV[:, 2 + tc, :], op0=ALU.mult, op1=ALU.mult))
                            P.op("vector", lambda v, hi=hi: v.tensor_scalar(
                                out=v16[:], in0=e2b[:], scalar1=vbc[:, hi:hi + 1],
                                scalar2=None, op0=ALU.add), inc="sFE")
                            pgv_free[bg] = P.n("sFE")
                            pgv_free[bv] = P.n("sFE")
                            slot = (2 * hi + tc) % 2
                            P.op("vector", lambda v, slot=slot: v.tensor_mul(
                                out=gvst[:, slot, :], in0=g16[:], in1=v16[:]),
                                waits=(("dGV%d" % slot, gvsp_at.get(gv_spill - 2, 0)),
                                       ("sBC", P.n("sBC"))), inc="sV2")
                            P.wait("vector", "sV2", P.n("sV2"))
                            P.op("vector", lambda v, tc=tc, slot=slot: v.tensor_tensor(
                                out=mmgx[:, tc, :], in0=mmgx[:, tc, :], in1=gvst[:, slot, :],
                                op=ALU.max),
                                waits=(("sQZ", mmgx_init),))
                            P.op("vector", lambda v, tc=tc, slot=slot: v.tensor_tensor(
                                out=mmgn[:, tc, :], in0=mmgn[:, tc, :], in1=gvst[:, slot, :],
                                op=ALU.min), inc="sFE")
                            gv_rdy = P.n("sFE")
                            P.op("gpsimd", lambda g, hi=hi, tc=tc, slot=slot: g.dma_start(
                                out=gv_d[hi][:, CH * tc:CH * (tc + 1)], in_=gvst[:, slot, :]),
                                waits=(("sFE", gv_rdy),), inc="dGV%d" % slot, inc_k=16)
                            gvsp_at[gv_spill] = P.n("dGV%d" % slot)
                            gv_spill += 1
                    for _ in range(2):
                        hio = ow_i
                        ow_i += 1
                        for hz in range(2):
                            kk = 2 * hio + hz
                            dsem = "dOW%d" % (kk % 2)
                            P.op("sync", lambda s, hio=hio, hz=hz, kk=kk: s.dma_start(
                                out=owst[:, kk % 2, :], in_=owTt[hio][:, 1024 * hz:1024 * (hz + 1)]),
                                waits=(("sO8", ow_dma_at.get(kk - 2, 0)),),
                                inc=dsem, inc_k=16)
                            P.op("vector", lambda v, kk=kk: v.tensor_scalar(
                                out=owf[:], in0=owst[:, kk % 2, :],
                                scalar1=colb[:, 42:43], scalar2=None,
                                op0=ALU.is_gt),
                                waits=((dsem, P.n(dsem)), ("sG", invg_rdy),
                                       ("dO8A", ow8_spill_at.get(kk - 1, 0))), inc="sL2")
                            P.op("vector", lambda v, kk=kk: v.tensor_scalar(
                                out=owf2[:], in0=owst[:, kk % 2, :],
                                scalar1=colb[:, 46:47], scalar2=None,
                                op0=ALU.is_lt),
                                waits=(("sL2", P.n("sL2")),), inc="sP2")
                            P.wait("vector", "sP2", P.n("sP2"))
                            P.op("vector", lambda v: v.tensor_tensor(
                                out=ow8b[:], in0=owf[:], in1=owf2[:],
                                op=ALU.subtract), inc="sO8")
                            ow_dma_at[kk] = P.n("sO8")
                            P.op("gpsimd", lambda g, hio=hio, hz=hz: g.dma_start(
                                out=ow8_d[hio][:, 1024 * hz:1024 * (hz + 1)], in_=ow8b[:]),
                                waits=(("sO8", P.n("sO8")),), inc="dO8A", inc_k=16)
                            ow8_spill_at[kk] = P.n("dO8A")
                if "gv" in taps:
                    P.wait("sync", "dGV0", P.n("dGV0"))
                    P.wait("sync", "dGV1", P.n("dGV1"))
                    P.op("sync", lambda s: s.dma_start(out=dbg["gv"], in_=gv_d), inc="sST", inc_k=16)
                    P.wait("sync", "sFT", P.n("sFT"))
                    P.op("sync", lambda s: s.dma_start(out=dbg["tern"][0], in_=terng[:]), inc="sST", inc_k=16)
                    P.op("sync", lambda s: s.dma_start(out=dbg["tern"][1], in_=ternv[:]), inc="sST", inc_k=16)

                P.op("vector", lambda v: v.tensor_scalar(
                    out=mmgn[:, 0, :], in0=mmgn[:, 0, :], scalar1=-1.0, scalar2=None,
                    op0=ALU.mult),
                    waits=(("sFE", P.n("sFE")),))
                P.op("vector", lambda v: v.tensor_scalar(
                    out=mmgn[:, 1, :], in0=mmgn[:, 1, :], scalar1=-1.0, scalar2=None,
                    op0=ALU.mult), inc="sQZ")
                P.op("gpsimd", lambda g: g.partition_all_reduce(
                    mmgx[:, 0, :], mmgx[:, 0, :], 128, ReduceOp.max),
                    waits=(("sQZ", P.n("sQZ")),))
                P.op("gpsimd", lambda g: g.partition_all_reduce(
                    mmgx[:, 1, :], mmgx[:, 1, :], 128, ReduceOp.max))
                P.op("gpsimd", lambda g: g.partition_all_reduce(
                    mmgn[:, 0, :], mmgn[:, 0, :], 128, ReduceOp.max))
                P.op("gpsimd", lambda g: g.partition_all_reduce(
                    mmgn[:, 1, :], mmgn[:, 1, :], 128, ReduceOp.max), inc="sG")
                P.op("vector", lambda v: v.tensor_tensor(
                    out=row(R_T1), in0=mmgx[0:1, 0, :], in1=mmgn[0:1, 0, :], op=ALU.max),
                    waits=(("sG", P.n("sG")),))
                P.op("vector", lambda v: v.tensor_tensor(
                    out=row(R_T2), in0=mmgx[0:1, 1, :], in1=mmgn[0:1, 1, :], op=ALU.max))
                P.op("vector", lambda v: v.tensor_scalar(
                    out=row(R_T1), in0=row(R_T1), scalar1=EPS, scalar2=None, op0=ALU.max))
                P.op("vector", lambda v: v.tensor_scalar(
                    out=row(R_T2), in0=row(R_T2), scalar1=EPS, scalar2=None, op0=ALU.max))
                for tc, src in ((0, R_T1), (1, R_T2)):
                    P.op("vector", lambda v, tc=tc, src=src: v.tensor_scalar(
                        out=row(R_SCO0 + 512 * tc), in0=row(src),
                        scalar1=rp[0:1, R_GAM + 2:R_GAM + 3], scalar2=1.0 / 127.0,
                        op0=ALU.mult, op1=ALU.mult))
                    P.op("vector", lambda v, tc=tc, src=src: v.reciprocal(
                        out=row(R_QB0 + 512 * tc), in_=row(src)))
                    P.op("vector", lambda v, tc=tc, src=src: v.tensor_mul(
                        out=row(R_T3), in0=row(src), in1=row(R_QB0 + 512 * tc)))
                    P.op("vector", lambda v: v.tensor_scalar(
                        out=row(R_T3), in0=row(R_T3), scalar1=-1.0, scalar2=2.0,
                        op0=ALU.mult, op1=ALU.add))
                    P.op("vector", lambda v, tc=tc: v.tensor_mul(
                        out=row(R_QB0 + 512 * tc), in0=row(R_QB0 + 512 * tc), in1=row(R_T3)))
                    P.op("vector", lambda v, tc=tc: v.tensor_scalar(
                        out=row(R_QB0 + 512 * tc), in0=row(R_QB0 + 512 * tc),
                        scalar1=127.0, scalar2=None, op0=ALU.mult), inc="sRW")
                gvrows_rdy = P.n("sRW")
                P.barrier()

        if _PHASES < 5:
            with nc.Block() as blk:
                P.replay(blk)
            return nc
        with ExitStack() as Ex:
            ee = Ex.enter_context
            gvq = ee(nc.sbuf_tensor("gvq", [128, NHI, 1024], BF16))
            outsb = ee(nc.sbuf_tensor("outsb", [128, 4, 1024], F32))
            gvst2 = ee(nc.sbuf_tensor("gvst2", [128, 2, 1024], FP16))
            o8b = ee(nc.sbuf_tensor("o8b", [128, 2, 1024], FP8))
            scO = ee(nc.sbuf_tensor("scO", [128, 2, CH], F32))
            qbB = ee(nc.sbuf_tensor("qbB", [128, 2, CH], F32))
            r1rb = ee(nc.sbuf_tensor("r1rb", [128, 2, CH], F32))
            qt1e = ee(nc.sbuf_tensor("qt1e", [128, CH], F32))
            qt2e = ee(nc.sbuf_tensor("qt2e", [128, CH], FP16))
            e1e = ee(nc.sbuf_tensor("e1e", [128, CH], F32))
            ofm = ee(nc.sbuf_tensor("ofm", [128, CH], F32))
            po = [ee(nc.psum_tensor(f"po{i}", [128, CH], F32)) for i in range(8)]

            for tc in range(2):
                P.op("gpsimd", lambda g, tc=tc: g.partition_broadcast(
                    scO[:, tc, :], rp[0:1, R_SCO0 + 512 * tc:R_SCO0 + 512 * tc + CH], 128),
                    waits=(("sRW", gvrows_rdy),))
                P.op("gpsimd", lambda g, tc=tc: g.partition_broadcast(
                    qbB[:, tc, :], rp[0:1, R_QB0 + 512 * tc:R_QB0 + 512 * tc + CH], 128),
                    inc="sG")
            scq_rdy = P.n("sG")

            qz_at = {}
            for hi in range(NHI):
                dsem = "dG2%d" % (hi % 2)
                P.op("gpsimd", lambda g, hi=hi: g.dma_start(
                    out=gvst2[:, hi % 2, :], in_=gv_d[hi]),
                    waits=(("dGV0", P.n("dGV0")), ("dGV1", P.n("dGV1")),
                           ("sQZ", qz_at.get(hi - 2, 0))),
                    inc=dsem, inc_k=16)
                gvdma = (dsem, P.n(dsem))
                for tc in range(2):
                    P.op("vector", lambda v, hi=hi, tc=tc: v.scalar_tensor_tensor(
                        out=qt1e[:], in0=gvst2[:, hi % 2, CH * tc:CH * (tc + 1)], scalar=1.0,
                        in1=qbB[:, tc, :], op0=ALU.mult, op1=ALU.mult),
                        waits=(gvdma, ("sG", scq_rdy)))
                    P.op("vector", lambda v: v.tensor_scalar(
                        out=qt2e[:], in0=qt1e[:], scalar1=MAGIC, scalar2=None, op0=ALU.add))
                    P.op("vector", lambda v: v.tensor_scalar(
                        out=qt2e[:], in0=qt2e[:], scalar1=MAGIC - 128.0,
                        scalar2=MAGIC + 127.0, op0=ALU.max, op1=ALU.min))
                    P.op("vector", lambda v, hi=hi, tc=tc: v.tensor_scalar(
                        out=gvq[:, hi, CH * tc:CH * (tc + 1)], in0=qt2e[:], scalar1=-MAGIC,
                        scalar2=None, op0=ALU.add), inc="sQZ")
                qz_at[hi] = P.n("sQZ")
            qz_all = P.n("sQZ")

            rounds = [] if _KP5 < 2 else [(0, 0), (1, 0), (0, 1), (1, 1)]
            rounds = rounds[:int(os.environ.get("KP5R", "4"))]
            pm_at = 0
            o8_at = {}
            oev_at = {}
            st_at = {}
            for rnd, (og, tc) in enumerate(rounds):
                for hi in range(NHI):
                    kk = rnd * NHI + hi
                    dsem = "dOB%d" % (kk % 2)
                    P.op("sync", lambda s, hi=hi, og=og, kk=kk: s.dma_start(
                        out=o8b[:, kk % 2, :], in_=ow8_d[hi][:, 1024 * og:1024 * (og + 1)]),
                        waits=(("dO8A", P.n("dO8A")),
                               ("sFM", o8_at.get(kk - 2, 0))),
                        inc=dsem, inc_k=16)
                    o8_now = (dsem, P.n(dsem))
                    for o in range(8):
                        oti = 8 * og + o
                        P.op("tensor", lambda t, hi=hi, o=o, tc=tc, kk=kk: t.matmul(
                            out=po[o][:], lhsT=o8b[:, kk % 2, 128 * o:128 * (o + 1)],
                            rhs=gvq[:, hi, CH * tc:CH * (tc + 1)],
                            start=(hi == 0), stop=(hi == NHI - 1)),
                            waits=(o8_now,
                                   ("sQZ", qz_at[hi] if rnd == 0 else 0),
                                   ("sOE", oev_at.get((rnd - 1, o), 0) if hi == 0 else 0)),
                            inc=("sFM" if o == 7 else None))
                    o8_at[kk] = P.n("sFM")
                mmo_done = P.n("sFM")
                for o in range(8):
                    oti = 8 * og + o
                    dsem = "dRB%d" % (o % 2)
                    P.op("sync", lambda s, tc=tc, oti=oti, o=o, rnd=rnd: s.dma_start(
                        out=r1rb[:, o % 2, :], in_=res1_d[tc, oti]),
                        waits=(("sOE", oev_at.get((rnd, o - 2), 0)),
                               ("dR10", P.n("dR10")), ("dR11", P.n("dR11"))),
                        inc=dsem, inc_k=16)
                    r1_now = (dsem, P.n(dsem))
                    if os.environ.get("KP5E"):
                        continue
                    P.op("vector", lambda v, o=o, tc=tc: v.scalar_tensor_tensor(
                        out=e1e[:], in0=po[o][:], scalar=1.0, in1=scO[:, tc, :],
                        op0=ALU.mult, op1=ALU.mult),
                        waits=(("sFM", mmo_done), ("sG", scq_rdy), r1_now))
                    P.op("vector", lambda v, o=o, oti=oti: v.scalar_tensor_tensor(
                        out=ofm[:], in0=e1e[:], scalar=obc[:, oti:oti + 1], in1=r1rb[:, o % 2, :],
                        op0=ALU.add, op1=ALU.add),
                        waits=(("sPM", pm_at),), inc="sEV")
                    ofm_rdy = P.n("sEV")
                    for q in range(4) if _KP5 >= 3 else []:
                        P.op("tensor", lambda t, o=o, q=q: t.transpose(
                            out=po[o][:, 128 * q:128 * (q + 1)],
                            in_=ofm[:, 128 * q:128 * (q + 1)], identity=identf[:]),
                            waits=(("sEV", ofm_rdy), ("sOE", P.n("sOE"))),
                            inc="sPM")
                        P.op("scalar", lambda s, o=o, q=q: s.activation(
                            out=outsb[:, q, 128 * o:128 * (o + 1)],
                            in_=po[o][:, 128 * q:128 * (q + 1)], func=AF.Copy,
                            bias=0.0, scale=1.0),
                            waits=(("sPM", P.n("sPM")),
                                   ("sST", st_at.get(rnd - 1, 0))),
                            inc="sOE")
                    oev_at[(rnd, o)] = P.n("sOE")
                    pm_at = P.n("sPM")
                for q in range(4) if _KP5 >= 4 else []:
                    P.op("sync", lambda s, tc=tc, og=og, q=q: s.dma_start(
                        out=out[tc, 128 * q:128 * (q + 1), 1024 * og:1024 * (og + 1)],
                        in_=outsb[:, q, :]),
                        waits=(("sOE", P.n("sOE")),), inc="sST", inc_k=16)
                st_at[rnd] = P.n("sST")
            P.wait("sync", "sST", P.n("sST"))
            P.wait("gpsimd", "sST", P.n("sST"))
            for s in ("dST0", "dST1", "dGT0", "dGT1", "dWP0", "dWP1",
                      "dXR0", "dXR1", "dQR0", "dQR1", "dRB0", "dRB1",
                      "dGW0", "dGW1", "dGW2", "dGW3", "dOW0", "dOW1",
                      "dOB0", "dOB1", "sAUX"):
                P.wait("sync", s, P.n(s))
            for s in ("dRC", "dR10", "dR11", "dGV0", "dGV1", "dO8A",
                      "dG20", "dG21", "sAGS"):
                P.wait("gpsimd", s, P.n(s))

        with nc.Block() as blk:
            P.replay(blk)

    return nc


_NC_CACHE = {}


def _get_nc(taps=()):
    key = tuple(sorted(taps))
    if key not in _NC_CACHE:
        nc = build_core(debug_taps=key)
        mybir.codegen_inst_isa_subclasses(nc)
        _NC_CACHE[key] = nc
    return _NC_CACHE[key]


def make_in_maps(L0_attn_x, L0_attn_W_qkv, L0_attn_W_proj, L0_ffn_gate_w,
                 L0_ffn_gate_b, L0_ffn_val_w, L0_ffn_val_b, L0_ffn_out_w,
                 L0_ffn_out_b):
    x = np.ascontiguousarray(np.asarray(L0_attn_x, dtype=np.float32))
    wqkv = np.asarray(L0_attn_W_qkv, dtype=np.float32)
    wp = np.asarray(L0_attn_W_proj, dtype=np.float32)
    gw = np.asarray(L0_ffn_gate_w, dtype=np.float32)
    gbv = np.asarray(L0_ffn_gate_b, dtype=np.float32)
    vw = np.asarray(L0_ffn_val_w, dtype=np.float32)
    vbv = np.asarray(L0_ffn_val_b, dtype=np.float32)
    ow = np.asarray(L0_ffn_out_w, dtype=np.float32)
    obv = np.asarray(L0_ffn_out_b, dtype=np.float32)

    wpT = np.ascontiguousarray(wp.T).reshape(NCI, 128, D)
    gwTt = np.ascontiguousarray(
        gw.T.reshape(NCI, 128, 16, 512).transpose(2, 0, 1, 3))
    vwTt = np.ascontiguousarray(
        vw.T.reshape(NCI, 128, 16, 512).transpose(2, 0, 1, 3))
    owTt = np.ascontiguousarray(ow.T).reshape(NHI, 128, D)
    gb_t = np.ascontiguousarray(gbv.reshape(NHI, 128).T)
    vb_t = np.ascontiguousarray(vbv.reshape(NHI, 128).T)
    ob_t = np.ascontiguousarray(obv.reshape(NCI, 128).T)

    in_maps = []
    for c in range(8):
        b, g = c // 4, c % 4
        xT = np.ascontiguousarray(x[b].T)
        xTt = np.ascontiguousarray(
            xT.reshape(NCI, 128, NCH, CH).transpose(2, 0, 1, 3))
        xs = x[:, CH * c:CH * (c + 1), :]
        xresi = np.ascontiguousarray(xs.transpose(0, 2, 1)).reshape(2, NCI, 128, CH)
        wsub = np.concatenate([wqkv[512 * g:512 * (g + 1)],
                               wqkv[D + 512 * g:D + 512 * (g + 1)],
                               wqkv[2 * D + 512 * g:2 * D + 512 * (g + 1)]], axis=0)
        wqkvT = np.ascontiguousarray(wsub.T).reshape(NCI, 128, QKVO)
        in_maps.append(dict(
            xTt=xTt, xresi=xresi, wqkvT=wqkvT, wpT=wpT, gwT=gwTt, vwT=vwTt,
            owTt=owTt, gpg=np.ascontiguousarray(gwTt[2 * c:2 * c + 2]),
            gpv=np.ascontiguousarray(vwTt[2 * c:2 * c + 2]),
            gpo=np.ascontiguousarray(owTt[8 * c:8 * c + 8]),
            gb_t=gb_t, vb_t=vb_t, ob_t=ob_t))
    return in_maps


def kernel(**inputs):
    in_maps = make_in_maps(**inputs)
    nc = _get_nc()
    res = run_bass_kernel_spmd(nc, in_maps, core_ids=list(range(8)))
    final = np.zeros((2, T, D), dtype=np.float32)
    for c in range(8):
        oc = np.asarray(res.results[c]["out"])
        final[:, CH * c:CH * (c + 1), :] = oc
    return final


# version 06 (final)
# speedup vs baseline: 1.0484x; 1.0484x over previous
import math
import os
from contextlib import ExitStack

_PHASES = int(os.environ.get("KPHASES", "5"))
_KP5 = int(os.environ.get("KP5", "9"))

import numpy as np
import ml_dtypes

import concourse.bass as bass
import concourse.mybir as mybir
from concourse.bass_utils import run_bass_kernel_spmd
from concourse.library_config import attn as _ATTN_LIB
from concourse.bass_isa import ReduceOp

F32 = mybir.dt.float32
BF16 = mybir.dt.bfloat16
FP16 = mybir.dt.float16
FP8 = mybir.dt.float8e4
AF = mybir.ActivationFunctionType
ALU = mybir.AluOpType
AX = mybir.AxisListType.X

D = 2048
T = 4096
NHC = 4
HID = 8192
CH = 512
NCH = T // CH
NCI = D // 128
QKVO = 3 * NHC * 128
NHI = HID // 128
ISD = 1.0 / math.sqrt(128.0)
MAGIC = 1536.0
EPS = 1e-5
NW = float(HID * D)
RG = [[0, 1, 2, 3, 4, 5, 6, 7]]

bfdt = ml_dtypes.bfloat16
ENGS = ("sync", "gpsimd", "scalar", "vector", "tensor")

R_MEAN, R_EX2, R_R, R_MR, R_T1, R_T2, R_T3 = 0, 512, 1024, 1536, 2048, 2560, 3072
R_SCA = 3584
R_GAM = 3600
R_SCO = 3608
R_SCO0, R_SCO1, R_QB0, R_QB1 = 0, 512, 1024, 1536


class Prog:

    def __init__(self, nc):
        self.nc = nc
        self.streams = {e: [] for e in ENGS}
        self.counts = {}
        self.sems = {}
        self.waited = {e: {} for e in ENGS}

    def sem(self, stack, name):
        self.sems[name] = stack.enter_context(self.nc.semaphore(name))
        self.counts[name] = 0

    def op(self, eng, fn, waits=(), inc=None, inc_k=1):
        w = []
        for s, c in waits:
            if c is None or c <= 0:
                continue
            if self.waited[eng].get(s, -1) >= c:
                continue
            self.waited[eng][s] = c
            w.append((s, c))
        if inc is not None:
            self.counts[inc] += inc_k
        self.streams[eng].append((fn, w, inc, inc_k))

    def wait(self, eng, sem, count=None):
        c = self.counts[sem] if count is None else count
        self.op(eng, None, waits=((sem, c),))

    def n(self, sem):
        return self.counts[sem]

    def barrier(self):
        for e in ENGS:
            self.op(e, None, inc="BAR")
        tgt = self.counts["BAR"]
        for e in ENGS:
            self.wait(e, "BAR", tgt)

    def replay(self, blk):
        for eng_name in ENGS:
            stream = self.streams[eng_name]

            def make(stream=stream):
                def run(eng):
                    for fn, waits, inc, inc_k in stream:
                        for s, c in waits:
                            eng.wait_ge(self.sems[s], c)
                        if fn is None:
                            if inc is not None:
                                eng.sem_inc(self.sems[inc], inc_k)
                            continue
                        ins = fn(eng)
                        if inc is not None:
                            ins.then_inc(self.sems[inc], inc_k)
                return run
            getattr(blk, eng_name)(make())


def _causal_masks():
    m = np.zeros((4, 128, CH), dtype=np.float32)
    for rel in range(4):
        for p in range(128):
            m[rel, p, 128 * rel + p:] = 1.0
    return m


def build_core(debug_taps=()):
    nc = bass.Bass("TRN2", target_bir_lowering=False, debug=False,
                   detect_race_conditions=False)
    taps = set(debug_taps)

    xTt = nc.dram_tensor("xTt", [NCH, NCI, 128, CH], F32, kind="ExternalInput").ap()
    xresi = nc.dram_tensor("xresi", [2, NCI, 128, CH], F32, kind="ExternalInput").ap()
    wqkvT = nc.dram_tensor("wqkvT", [NCI, 128, QKVO], F32, kind="ExternalInput").ap()
    wpT = nc.dram_tensor("wpT", [NCI, 128, D], F32, kind="ExternalInput").ap()
    gwT = nc.dram_tensor("gwT", [16, NCI, 128, 512], F32, kind="ExternalInput").ap()
    vwT = nc.dram_tensor("vwT", [16, NCI, 128, 512], F32, kind="ExternalInput").ap()
    owTt = nc.dram_tensor("owTt", [NHI, 128, D], F32, kind="ExternalInput").ap()
    gpg = nc.dram_tensor("gpg", [2, NCI, 128, 512], F32, kind="ExternalInput").ap()
    gpv = nc.dram_tensor("gpv", [2, NCI, 128, 512], F32, kind="ExternalInput").ap()
    gpo = nc.dram_tensor("gpo", [8, 128, D], F32, kind="ExternalInput").ap()
    gb_t = nc.dram_tensor("gb_t", [128, NHI], F32, kind="ExternalInput").ap()
    vb_t = nc.dram_tensor("vb_t", [128, NHI], F32, kind="ExternalInput").ap()
    ob_t = nc.dram_tensor("ob_t", [128, NCI], F32, kind="ExternalInput").ap()
    out = nc.dram_tensor("out", [2, CH, D], F32, kind="ExternalOutput").ap()

    ag_in = [nc.dram_tensor(f"ag_in{h}", [128, T], BF16, kind="Internal") for h in range(NHC)]
    ag_out = [nc.dram_tensor(f"ag_out{h}", [8 * 128, T], BF16, kind="Internal") for h in range(NHC)]
    gp_in = nc.dram_tensor("gp_in", [1, 4], F32, kind="Internal")
    gp_out = nc.dram_tensor("gp_out", [8, 4], F32, kind="Internal")
    res1_d = nc.dram_tensor("res1_d", [2, NCI, 128, CH], F32, kind="Internal").ap()
    gv_d = nc.dram_tensor("gv_d", [NHI, 128, 1024], FP16, kind="Internal").ap()
    ow8_d = nc.dram_tensor("ow8_d", [NHI, 128, D], FP8, kind="Internal").ap()

    masks_t = nc.inline_tensor(_causal_masks().reshape(4 * 128, CH).astype(bfdt), name="masksc")
    identf_t = nc.inline_tensor(np.eye(128, dtype=np.float32), name="identfc")
    onesc_t = nc.inline_tensor(np.ones((128, 1), np.float32).astype(bfdt), name="onescc")
    onesr_t = nc.inline_tensor(np.ones((1, 128), np.float32).astype(bfdt), name="onesrc")

    dbg = {}
    if "qkv" in taps:
        dbg["q"] = nc.dram_tensor("dbg_q", [128, NHC, T], BF16, kind="ExternalOutput").ap()
        dbg["k"] = nc.dram_tensor("dbg_k", [128, NHC, T], BF16, kind="ExternalOutput").ap()
        dbg["v"] = nc.dram_tensor("dbg_v", [128, 32, 512], BF16, kind="ExternalOutput").ap()
        dbg["brb"] = nc.dram_tensor("dbg_brb", [128, 2, CH], BF16, kind="ExternalOutput").ap()
        dbg["colb"] = nc.dram_tensor("dbg_colb", [128, 64], F32, kind="ExternalOutput").ap()
    if "y" in taps:
        dbg["y"] = nc.dram_tensor("dbg_y", [128, NHC, T], BF16, kind="ExternalOutput").ap()
    if "rows" in taps:
        dbg["rows"] = nc.dram_tensor("dbg_rows", [1, 4096], F32, kind="ExternalOutput").ap()
        dbg["rows16"] = nc.dram_tensor("dbg_rows16", [1, 1024], BF16, kind="ExternalOutput").ap()
        dbg["pst"] = nc.dram_tensor("dbg_pst", [1, 2 * CH], F32, kind="ExternalOutput").ap()
    if "res1" in taps:
        dbg["res1"] = nc.dram_tensor("dbg_res1", [2, NCI, 128, CH], F32, kind="ExternalOutput").ap()
    if "xq" in taps:
        dbg["xq"] = nc.dram_tensor("dbg_xq", [128, NCI, 1024], BF16, kind="ExternalOutput").ap()
        dbg["ab"] = nc.dram_tensor("dbg_ab", [1, 4 * CH], F32, kind="ExternalOutput").ap()
    if "gv" in taps:
        dbg["gv"] = nc.dram_tensor("dbg_gv", [NHI, 128, 1024], FP16, kind="ExternalOutput").ap()
        dbg["tern"] = nc.dram_tensor("dbg_tern", [2, 128, 2, NCI, 256], BF16, kind="ExternalOutput").ap()
        dbg["gam"] = nc.dram_tensor("dbg_gam", [1, 80], F32, kind="ExternalOutput").ap()

    P = Prog(nc)

    with ExitStack() as top:
        et = top.enter_context
        identf = et(nc.sbuf_tensor("identf", [128, 128], F32))
        onesc = et(nc.sbuf_tensor("onesc", [128, 1], BF16))
        onesr = et(nc.sbuf_tensor("onesr", [1, 128], BF16))
        gbc = et(nc.sbuf_tensor("gbc", [128, NHI], F32))
        vbc = et(nc.sbuf_tensor("vbc", [128, NHI], F32))
        obc = et(nc.sbuf_tensor("obc", [128, NCI], F32))
        rp = et(nc.sbuf_tensor("rp", [128, 4096], F32))
        rp16 = et(nc.sbuf_tensor("rp16", [128, 1024], BF16))
        colb = et(nc.sbuf_tensor("colb", [128, 64], F32))
        cmagic = et(nc.sbuf_tensor("cmagic", [128, 1], F32))
        ceps = et(nc.sbuf_tensor("ceps", [128, 1], F32))

        for s in ("BAR", "sAUX", "sSTG", "sSM", "sSMt", "sRW",
                  "sBC", "sQK", "sEV", "sS", "sP", "sPM", "sAV", "sYE", "sG",
                  "sAGD", "sAGS", "sGP", "sWP", "sPJ", "sL2",
                  "sL2T", "sQZ", "sFT", "sFM", "sFE",
                  "sO8", "sOE", "sST", "sP2", "sV2",
                  "dST0", "dST1", "dGT0", "dGT1", "dWP0", "dWP1", "dRC",
                  "dXR0", "dXR1", "dR10", "dR11", "dQR0", "dQR1",
                  "dRB0", "dRB1", "dGW0", "dGW1", "dGW2", "dGW3",
                  "dOW0", "dOW1", "dGV0", "dGV1", "dO8A", "dG20", "dG21",
                  "dOB0", "dOB1"):
            P.sem(top, s)

        def row(a, n=CH):
            return rp[0:1, a:a + n]

        def row16(a, n=CH):
            return rp16[0:1, a:a + n]

        P.op("sync", lambda s: s.dma_start(out=identf[:], in_=identf_t.ap()), inc="sAUX", inc_k=16)
        P.op("sync", lambda s: s.dma_start(out=onesc[:], in_=onesc_t.ap()), inc="sAUX", inc_k=16)
        P.op("sync", lambda s: s.dma_start(out=onesr[:], in_=onesr_t.ap()), inc="sAUX", inc_k=16)
        P.op("sync", lambda s: s.dma_start(out=gbc[:], in_=gb_t[:]), inc="sAUX", inc_k=16)
        P.op("sync", lambda s: s.dma_start(out=vbc[:], in_=vb_t[:]), inc="sAUX", inc_k=16)
        P.op("sync", lambda s: s.dma_start(out=obc[:], in_=ob_t[:]), inc="sAUX", inc_k=16)
        P.op("gpsimd", lambda g: g.load_library(_ATTN_LIB))
        P.op("scalar", lambda s: s.preload_act_table(AF.Sqrt))
        P.op("vector", lambda v: v.memset(colb[:], 0.0))
        P.op("vector", lambda v: v.memset(rp[:], 0.0))
        P.op("vector", lambda v: v.memset(cmagic[:], MAGIC))
        P.op("vector", lambda v: v.memset(ceps[:], EPS), inc="sAUX", inc_k=16)
        aux_all = P.n("sAUX")

        with ExitStack() as SATT:
            es = SATT.enter_context
            qb = es(nc.sbuf_tensor("qb", [128, NHC, T], BF16))
            kb = es(nc.sbuf_tensor("kb", [128, NHC, T], BF16))
            vbuf = es(nc.sbuf_tensor("vbuf", [128, 32, 512], BF16))

            with ExitStack() as A:
                ea = A.enter_context
                wq = ea(nc.sbuf_tensor("wq", [128, NCI, QKVO], BF16))
                xbuf = ea(nc.sbuf_tensor("xbuf", [128, NCI, CH], BF16))
                stage = ea(nc.sbuf_tensor("stage", [128, 2, CH], F32))
                xsq = ea(nc.sbuf_tensor("xsq", [128, 2, CH], BF16))
                brB = ea(nc.sbuf_tensor("brB", [128, 2, CH], BF16))
                svB = ea(nc.sbuf_tensor("svB", [128, 512], BF16))
                t1b = ea(nc.sbuf_tensor("t1b", [128, CH], F32))
                pstat = ea(nc.psum_tensor("pstat", [1, CH], F32))
                pstat2 = ea(nc.psum_tensor("pstat2", [1, CH], F32))
                pbc = ea(nc.psum_tensor("pbc", [128, CH], F32))
                pqk = [ea(nc.psum_tensor(f"pqk{i}", [128, CH], F32)) for i in range(4)]
                ptc = ea(nc.psum_tensor("ptc", [128, 16], F32))

                NWP = QKVO // CH
                for ci in range(NCI):
                    for pz in range(NWP):
                        kk = ci * NWP + pz
                        dsem = "dST%d" % (kk % 2)
                        P.op("sync", lambda s, ci=ci, pz=pz, kk=kk: s.dma_start(
                            out=stage[:, kk % 2, :], in_=wqkvT[ci][:, CH * pz:CH * (pz + 1)]),
                            waits=(("sSTG", kk - 1),), inc=dsem, inc_k=16)
                        P.op("scalar", lambda s, ci=ci, pz=pz, kk=kk: s.activation(
                            out=wq[:, ci, CH * pz:CH * (pz + 1)], in_=stage[:, kk % 2, :],
                            func=AF.Copy, bias=0.0, scale=1.0),
                            waits=((dsem, P.n(dsem)),), inc="sSTG")
                wq_rdy = P.n("sSTG")

                P.wait("tensor", "sAUX", aux_all)
                for pz in range(NWP):
                    for ci in range(NCI):
                        P.op("tensor", lambda t, pz=pz, ci=ci: t.matmul(
                            out=pstat[:], lhsT=onesc[:], rhs=wq[:, ci, CH * pz:CH * (pz + 1)],
                            start=(ci == 0), stop=(ci == NCI - 1)),
                            waits=(("sSTG", wq_rdy),),
                            inc=("sSM" if ci == NCI - 1 else None))
                    P.op("vector", lambda v: v.tensor_copy(out=row(R_T1), in_=pstat[:]),
                         waits=(("sSM", P.n("sSM")),), inc="sRW")
                    if pz < 2:
                        for qq in range(4):
                            P.op("tensor", lambda t, qq=qq: t.transpose(
                                out=ptc[:, qq:qq + 1],
                                in_=rp[0:1, R_T1 + 128 * qq:R_T1 + 128 * (qq + 1)],
                                identity=identf[0:1, 0:1]),
                                waits=(("sRW", P.n("sRW")),),
                                inc=("sSM" if qq == 3 else None))
                        P.op("vector", lambda v, pz=pz: v.tensor_copy(
                            out=colb[:, 4 * pz:4 * (pz + 1)], in_=ptc[:, 0:4]),
                            waits=(("sSM", P.n("sSM")),), inc="sRW")
                    else:
                        P.op("vector", lambda v: v.tensor_copy(out=row16(0), in_=row(R_T1)),
                             inc="sRW")
                        P.op("tensor", lambda t: t.matmul(
                            out=pbc[:], lhsT=onesr[:], rhs=row16(0), start=True, stop=True),
                            waits=(("sRW", P.n("sRW")),), inc="sSM")
                        P.op("scalar", lambda s: s.activation(
                            out=svB[:], in_=pbc[:], func=AF.Copy, bias=0.0, scale=1.0),
                            waits=(("sSM", P.n("sSM")),), inc="sBC")
                svb_rdy = P.n("sBC")
                bc_free = P.n("sBC")
                stat_free = P.n("sRW")
                ev_at = {i: 0 for i in range(4)}
                chunk_qk_done = 0
                chunk_ev_done = 0

                for ch in range(NCH):
                    for ci in range(NCI):
                        kk = ch * NCI + ci
                        dsem = "dST%d" % (kk % 2)
                        P.op("sync", lambda s, ch=ch, ci=ci, kk=kk: s.dma_start(
                            out=stage[:, kk % 2, :], in_=xTt[ch, ci]),
                            waits=(("sSTG", wq_rdy + kk - 1),), inc=dsem, inc_k=16)
                        P.op("scalar", lambda s, ci=ci, kk=kk: s.activation(
                            out=xbuf[:, ci, :], in_=stage[:, kk % 2, :], func=AF.Copy,
                            bias=0.0, scale=1.0),
                            waits=((dsem, P.n(dsem)), ("sQK", chunk_qk_done)))
                        P.op("scalar", lambda s, ci=ci, kk=kk: s.activation(
                            out=xsq[:, kk % 2, :], in_=stage[:, kk % 2, :], func=AF.Square,
                            bias=0.0, scale=1.0),
                            waits=(("sSMt", kk - 1),), inc="sSTG")
                        P.op("tensor", lambda t, ci=ci: t.matmul(
                            out=pstat[:], lhsT=onesc[:], rhs=xbuf[:, ci, :],
                            start=(ci == 0), stop=(ci == NCI - 1)),
                            waits=(("sSTG", P.n("sSTG")),
                                   ("sRW", stat_free if ci == 0 else 0)))
                        P.op("tensor", lambda t, ci=ci, kk=kk: t.matmul(
                            out=pstat2[:], lhsT=onesc[:], rhs=xsq[:, kk % 2, :],
                            start=(ci == 0), stop=(ci == NCI - 1)),
                            inc="sSMt")
                    stats_done = P.n("sSMt")

                    P.op("vector", lambda v: v.tensor_scalar(
                        out=row(R_MEAN), in0=pstat[:], scalar1=1.0 / D, scalar2=None,
                        op0=ALU.mult),
                        waits=(("sSMt", stats_done),))
                    P.op("vector", lambda v: v.tensor_scalar(
                        out=row(R_EX2), in0=pstat2[:], scalar1=1.0 / D, scalar2=None,
                        op0=ALU.mult), inc="sRW")
                    stat_free = P.n("sRW")
                    P.op("vector", lambda v: v.tensor_mul(out=row(R_T1), in0=row(R_MEAN),
                                                          in1=row(R_MEAN)))
                    P.op("vector", lambda v: v.tensor_sub(out=row(R_EX2), in0=row(R_EX2),
                                                          in1=row(R_T1)), inc="sRW")
                    P.op("vector", lambda v: v.tensor_scalar(
                        out=row(R_T2), in0=row(R_EX2), scalar1=EPS, scalar2=None,
                        op0=ALU.add))
                    P.op("vector", lambda v: v.reciprocal(out=row(R_T2), in_=row(R_T2)),
                         inc="sRW")
                    P.op("scalar", lambda s: s.activation(out=row(R_R), in_=row(R_T2),
                                                          func=AF.Sqrt, bias=0.0, scale=1.0),
                         waits=(("sRW", P.n("sRW")),), inc="sBC")
                    P.op("vector", lambda v: v.tensor_mul(out=row(R_MR), in0=row(R_MEAN),
                                                          in1=row(R_R)),
                         waits=(("sBC", P.n("sBC")),))
                    P.op("vector", lambda v: v.tensor_scalar(
                        out=row16(0), in0=row(R_R), scalar1=-1.0, scalar2=None, op0=ALU.mult))
                    P.op("vector", lambda v: v.tensor_copy(out=row16(512), in_=row(R_MR)),
                         inc="sRW")
                    rows_b16 = P.n("sRW")
                    if "rows" in taps and ch == 0:
                        P.wait("sync", "sRW", rows_b16)
                        P.op("sync", lambda s: s.dma_start(
                            out=dbg["rows"], in_=rp[0:1, :]), inc="sST", inc_k=16)
                        P.op("sync", lambda s: s.dma_start(
                            out=dbg["rows16"], in_=rp16[0:1, :]), inc="sST", inc_k=16)

                    P.op("tensor", lambda t: t.matmul(out=pbc[:], lhsT=onesr[:], rhs=row16(0),
                                                      start=True, stop=True),
                         waits=(("sRW", rows_b16), ("sBC", bc_free)), inc="sSM")
                    P.op("scalar", lambda s: s.activation(out=brB[:, 0, :], in_=pbc[:],
                                                          func=AF.Copy, bias=0.0, scale=1.0),
                         waits=(("sEV", chunk_ev_done), ("sSM", P.n("sSM"))), inc="sBC")
                    P.op("tensor", lambda t: t.matmul(out=pbc[:], lhsT=onesr[:], rhs=row16(512),
                                                      start=True, stop=True),
                         waits=(("sBC", P.n("sBC")),), inc="sSM")
                    P.op("scalar", lambda s: s.activation(out=brB[:, 1, :], in_=pbc[:],
                                                          func=AF.Copy, bias=0.0, scale=1.0),
                         waits=(("sSM", P.n("sSM")),), inc="sBC")
                    brb_rdy = P.n("sBC")
                    bc_free = brb_rdy
                    for tt in range(4):
                        P.op("tensor", lambda t, tt=tt: t.transpose(
                            out=ptc[:, 2 * tt:2 * tt + 1],
                            in_=rp[0:1, R_R + 128 * tt:R_R + 128 * (tt + 1)],
                            identity=identf[0:1, 0:1]),
                            waits=(("sRW", rows_b16),))
                        P.op("tensor", lambda t, tt=tt: t.transpose(
                            out=ptc[:, 2 * tt + 1:2 * tt + 2],
                            in_=rp[0:1, R_MR + 128 * tt:R_MR + 128 * (tt + 1)],
                            identity=identf[0:1, 0:1]),
                            inc=("sSM" if tt == 3 else None))
                    P.op("vector", lambda v: v.tensor_scalar(
                        out=colb[:, 24:28], in0=ptc[:, 0:8:2], scalar1=-1.0, scalar2=None,
                        op0=ALU.mult),
                        waits=(("sSM", P.n("sSM")),))
                    P.op("vector", lambda v: v.tensor_copy(out=colb[:, 16:24], in_=ptc[:, 0:8]),
                         inc="sRW")
                    cols_rdy = P.n("sRW")

                    for grp in range(2):
                        base = 512 * grp
                        dstb = qb if grp == 0 else kb
                        for ot in range(4):
                            for ci in range(NCI):
                                P.op("tensor", lambda t, ot=ot, ci=ci, base=base: t.matmul(
                                    out=pqk[ot][:],
                                    lhsT=wq[:, ci, base + 128 * ot:base + 128 * (ot + 1)],
                                    rhs=xbuf[:, ci, :],
                                    start=(ci == 0), stop=(ci == NCI - 1)),
                                    waits=(("sEV", ev_at[ot] if ci == 0 else 0),),
                                    inc=("sQK" if ci == NCI - 1 else None))
                            gdone = P.n("sQK")
                            if os.environ.get("KQRAW"):
                                P.op("vector", lambda v, ot=ot, dstb=dstb, ch=ch: v.tensor_copy(
                                    out=dstb[:, ot, CH * ch:CH * (ch + 1)], in_=pqk[ot][:]),
                                    waits=(("sQK", gdone), ("sBC", brb_rdy)), inc="sEV")
                                ev_at[ot] = P.n("sEV")
                                continue
                            P.op("vector", lambda v, ot=ot, grp=grp: v.scalar_tensor_tensor(
                                out=t1b[:], in0=brB[:, 1, :],
                                scalar=colb[:, 4 * grp + ot:4 * grp + ot + 1],
                                in1=pqk[ot][:], op0=ALU.mult, op1=ALU.subtract),
                                waits=(("sQK", gdone), ("sBC", brb_rdy)))
                            P.op("vector", lambda v, ot=ot, dstb=dstb, ch=ch: v.tensor_mul(
                                out=dstb[:, ot, CH * ch:CH * (ch + 1)], in0=t1b[:],
                                in1=brB[:, 0, :]), inc="sEV")
                            ev_at[ot] = P.n("sEV")
                    for tt in range(4):
                        for ci in range(NCI):
                            P.op("tensor", lambda t, tt=tt, ci=ci: t.matmul(
                                out=pqk[tt][:], lhsT=xbuf[:, ci, 128 * tt:128 * (tt + 1)],
                                rhs=wq[:, ci, 1024:1536],
                                start=(ci == 0), stop=(ci == NCI - 1)),
                                waits=(("sEV", ev_at[tt] if ci == 0 else 0),),
                                inc=("sQK" if ci == NCI - 1 else None))
                        vdone = P.n("sQK")
                        P.op("vector", lambda v, tt=tt: v.scalar_tensor_tensor(
                            out=t1b[:], in0=svB[:], scalar=colb[:, 17 + 2 * tt:18 + 2 * tt],
                            in1=pqk[tt][:], op0=ALU.mult, op1=ALU.subtract),
                            waits=(("sQK", vdone), ("sRW", cols_rdy), ("sBC", svb_rdy)))
                        P.op("vector", lambda v, tt=tt, ch=ch: v.tensor_scalar(
                            out=vbuf[:, 4 * ch + tt, :], in0=t1b[:],
                            scalar1=colb[:, 24 + tt:25 + tt], scalar2=None, op0=ALU.mult),
                            inc="sEV")
                        ev_at[tt] = P.n("sEV")
                    chunk_qk_done = P.n("sQK")
                    chunk_ev_done = P.n("sEV")

                qkv_done = P.n("sEV")
                if "qkv" in taps:
                    P.wait("sync", "sEV", qkv_done)
                    P.op("sync", lambda s: s.dma_start(out=dbg["brb"], in_=brB[:]), inc="sST", inc_k=16)
                    P.op("sync", lambda s: s.dma_start(out=dbg["colb"], in_=colb[:]), inc="sST", inc_k=16)
                    P.op("sync", lambda s: s.dma_start(out=dbg["q"], in_=qb[:]), inc="sST", inc_k=16)
                    P.op("sync", lambda s: s.dma_start(out=dbg["k"], in_=kb[:]), inc="sST", inc_k=16)
                    P.op("sync", lambda s: s.dma_start(out=dbg["v"], in_=vbuf[:]), inc="sST", inc_k=16)
                P.barrier()

            if _PHASES < 2:
                with nc.Block() as blk:
                    P.replay(blk)
                return nc
            with ExitStack() as B:
                eb = B.enter_context
                masks = eb(nc.sbuf_tensor("masks", [128, 4, CH], BF16))
                yh = eb(nc.sbuf_tensor("yh", [128, 2, T], BF16))
                pexp = eb(nc.sbuf_tensor("pexp", [128, 4, CH], BF16))
                qsqb = eb(nc.sbuf_tensor("qsqb", [128, T], BF16))
                gst = eb(nc.sbuf_tensor("gst", [128, 2, D], F32))
                gacc = eb(nc.sbuf_tensor("gacc", [128, 80], F32))
                gsc = eb(nc.sbuf_tensor("gsc", [128, 16], F32))
                invB = eb(nc.sbuf_tensor("invB", [128, 2, CH], F32))
                ps = [eb(nc.psum_tensor(f"ps{i}", [128, CH], F32)) for i in range(2)]
                py = [eb(nc.psum_tensor(f"py{i}", [128, CH], F32)) for i in range(2)]
                prs = [eb(nc.psum_tensor(f"prs{i}", [1, CH], F32)) for i in range(2)]
                prow = eb(nc.psum_tensor("prow", [1, CH], F32))
                pinv = eb(nc.psum_tensor("pinv", [128, CH], F32))

                P.op("scalar", lambda s: s.preload_act_table(AF.Exp))
                P.op("sync", lambda s: s.dma_start(
                    out=masks[:], in_=masks_t.ap().rearrange("(f p) t -> p f t", p=128)),
                    inc="sAUX", inc_k=16)
                masks_rdy = P.n("sAUX")

                gtiles = []
                for w in range(2):
                    src = gpg if w == 0 else gpv
                    for hcs in range(2):
                        for ci in range(NCI):
                            gtiles.append((src[hcs, ci], CH))
                for hb in range(8):
                    gtiles.append((gpo[hb], D))
                for i, (ap, fd) in enumerate(gtiles):
                    dsem = "dGT%d" % (i % 2)
                    P.op("sync", lambda s, ap=ap, i=i, fd=fd: s.dma_start(
                        out=gst[:, i % 2, 0:fd], in_=ap),
                        waits=(("sGP", 2 * (i - 1)),), inc=dsem, inc_k=16)
                    P.op("vector", lambda v, i=i, fd=fd: v.tensor_reduce(
                        out=gsc[:, 0:16].rearrange("p (a b) -> p a b", a=16),
                        in_=gst[:, i % 2, 0:fd].rearrange("p (a b) -> p a b", a=16),
                        axis=AX, op=ALU.add, apply_absolute_value=True),
                        waits=((dsem, P.n(dsem)),), inc="sGP")
                    P.wait("vector", "sGP", P.n("sGP"))
                    P.op("vector", lambda v, i=i: v.tensor_reduce(
                        out=gacc[:, i:i + 1], in_=gsc[:, 0:16], axis=AX,
                        op=ALU.add),
                        inc="sGP")
                ngt = len(gtiles)
                P.op("vector", lambda v: v.tensor_reduce(
                    out=gacc[:, 72:73], in_=gacc[:, 0:32], axis=AX, op=ALU.add),
                    waits=(("sGP", 2 * ngt),))
                P.op("vector", lambda v: v.tensor_reduce(
                    out=gacc[:, 73:74], in_=gacc[:, 32:64], axis=AX, op=ALU.add))
                P.op("vector", lambda v: v.tensor_reduce(
                    out=gacc[:, 74:75], in_=gacc[:, 64:72], axis=AX, op=ALU.add))
                P.op("vector", lambda v: v.memset(gacc[:, 75:76], 0.0), inc="sGP")
                P.op("gpsimd", lambda g: g.partition_all_reduce(
                    gacc[:, 76:80], gacc[:, 72:76], 128, ReduceOp.add),
                    waits=(("sGP", P.n("sGP")),), inc="sG")
                P.op("gpsimd", lambda g: g.dma_start(out=gp_in.ap(), in_=gacc[0:1, 76:80]),
                     inc="sAGS", inc_k=16)
                P.op("gpsimd", lambda g: g.collective_compute(
                    "AllGather", ALU.bypass, RG, ins=[gp_in.ap()], outs=[gp_out.ap()]),
                    waits=(("sAGS", P.n("sAGS")),), inc="sAGD")
                gp_agd = P.n("sAGD")
                P.op("gpsimd", lambda g: g.dma_start(
                    out=gacc[0:1, 0:32].rearrange("p (a b) -> p a b", a=4),
                    in_=gp_out.ap().transpose([1, 0]),
                    allow_slow_non_contiguous=True),
                    waits=(("sAGD", gp_agd),), inc="sAGS", inc_k=16)
                P.op("vector", lambda v: v.tensor_reduce(
                    out=gacc[0:1, 32:36].rearrange("p (a b) -> p a b", a=4),
                    in_=gacc[0:1, 0:32].rearrange("p (a b) -> p a b", a=4),
                    axis=AX, op=ALU.add),
                    waits=(("sAGS", P.n("sAGS")),), inc="sGP")
                P.wait("vector", "sGP", P.n("sGP"))
                P.op("vector", lambda v: v.tensor_scalar(
                    out=gacc[0:1, 36:40], in0=gacc[0:1, 32:36], scalar1=1.0 / NW,
                    scalar2=EPS, op0=ALU.mult, op1=ALU.max), inc="sGP")
                P.wait("vector", "sGP", P.n("sGP"))
                P.op("vector", lambda v: v.reciprocal(out=gacc[0:1, 40:44], in_=gacc[0:1, 36:40]),
                     inc="sGP")
                P.wait("vector", "sGP", P.n("sGP"))
                P.op("vector", lambda v: v.tensor_mul(out=gacc[0:1, 44:48], in0=gacc[0:1, 36:40],
                                                      in1=gacc[0:1, 40:44]), inc="sGP")
                P.wait("vector", "sGP", P.n("sGP"))
                P.op("vector", lambda v: v.tensor_scalar(
                    out=gacc[0:1, 44:48], in0=gacc[0:1, 44:48], scalar1=-1.0, scalar2=2.0,
                    op0=ALU.mult, op1=ALU.add), inc="sGP")
                P.wait("vector", "sGP", P.n("sGP"))
                P.op("vector", lambda v: v.tensor_mul(out=gacc[0:1, 40:44], in0=gacc[0:1, 40:44],
                                                      in1=gacc[0:1, 44:48]), inc="sGP")
                P.wait("vector", "sGP", P.n("sGP"))
                P.op("vector", lambda v: v.tensor_scalar(
                    out=gacc[0:1, 48:52], in0=gacc[0:1, 36:40], scalar1=0.5, scalar2=None,
                    op0=ALU.mult))
                P.op("vector", lambda v: v.tensor_scalar(
                    out=gacc[0:1, 52:56], in0=gacc[0:1, 36:40], scalar1=-0.5, scalar2=None,
                    op0=ALU.mult), inc="sGP")
                P.wait("vector", "sGP", P.n("sGP"))
                P.op("vector", lambda v: v.tensor_copy(out=rp[0:1, R_GAM:R_GAM + 8],
                                                       in_=gacc[0:1, 36:44]), inc="sGP")
                gam_rdy = P.n("sGP")
                P.op("gpsimd", lambda g: g.partition_broadcast(colb[:, 32:36], gacc[0:1, 40:44], 128),
                     waits=(("sGP", gam_rdy),), inc="sG")
                P.op("gpsimd", lambda g: g.partition_broadcast(colb[:, 40:44], gacc[0:1, 48:52], 128))
                P.op("gpsimd", lambda g: g.partition_broadcast(colb[:, 44:48], gacc[0:1, 52:56], 128),
                     inc="sG")
                invg_rdy = P.n("sG")
                if "gv" in taps:
                    P.wait("sync", "sGP", gam_rdy)
                    P.op("sync", lambda s: s.dma_start(out=dbg["gam"], in_=gacc[0:1, 0:80]),
                         inc="sST", inc_k=16)

                yev_at = [0, 0]
                rsev_at = [0, 0]
                agdma_at = [0, 0]
                invev_at = [0, 0]
                exp_hist = []
                av_hist = []
                negb_at = [0, 0]
                exp_at_head = [0, 0, 0, 0]

                def emit_bound(hh):
                    for which, buf in ((0, qb), (1, kb)):
                        P.op("scalar", lambda s, buf=buf, hh=hh: s.activation(
                            out=qsqb[:], in_=buf[:, hh, :], func=AF.Square, bias=0.0,
                            scale=1.0),
                            waits=(("sS", P.n("sS")),), inc="sPM")
                        sq_done = P.n("sPM")
                        for j in range(NCH):
                            P.op("tensor", lambda t, j=j: t.matmul(
                                out=prow[:], lhsT=onesc[:], rhs=qsqb[:, CH * j:CH * (j + 1)],
                                start=True, stop=True),
                                waits=(("sPM", sq_done), ("sRW", P.n("sRW"))),
                                inc="sS")
                            P.op("vector", lambda v, j=j: v.tensor_reduce(
                                out=rp[0:1, R_SCA + j:R_SCA + j + 1], in_=prow[:], axis=AX,
                                op=ALU.max),
                                waits=(("sS", P.n("sS")),), inc="sRW")
                        P.op("vector", lambda v, which=which: v.tensor_reduce(
                            out=rp[0:1, R_SCA + 8 + which:R_SCA + 9 + which],
                            in_=rp[0:1, R_SCA:R_SCA + 8], axis=AX, op=ALU.max),
                            inc="sRW")
                    P.op("vector", lambda v: v.tensor_add(
                        out=rp[0:1, R_SCA + 10:R_SCA + 11], in0=rp[0:1, R_SCA + 8:R_SCA + 9],
                        in1=rp[0:1, R_SCA + 9:R_SCA + 10]))
                    P.op("vector", lambda v: v.tensor_scalar(
                        out=rp[0:1, R_SCA + 11:R_SCA + 12], in0=rp[0:1, R_SCA + 10:R_SCA + 11],
                        scalar1=-0.5 * ISD, scalar2=None, op0=ALU.mult),
                        waits=(("sG", negb_at[(hh - 1) % 2]),), inc="sRW")
                    P.op("gpsimd", lambda g, hh=hh: g.partition_broadcast(
                        colb[:, 36 + hh % 2:37 + hh % 2], rp[0:1, R_SCA + 11:R_SCA + 12], 128),
                        waits=(("sRW", P.n("sRW")),
                               ("sP", exp_at_head[hh - 2] if hh >= 2 else 0)),
                        inc="sG")
                    negb_at[hh % 2] = P.n("sG")

                emit_bound(0)
                for h in range(NHC):
                    negb_rdy = negb_at[h % 2]
                    for j in range(NCH):
                        ntk = 4 * (j + 1)
                        pysel = py[j % 2]
                        prssel = prs[j % 2]
                        for tk in range(ntk):
                            gi = len(exp_hist)
                            P.op("tensor", lambda t, h=h, j=j, tk=tk: t.matmul(
                                out=ps[tk % 2][:],
                                lhsT=kb[:, h, 128 * tk:128 * (tk + 1)],
                                rhs=qb[:, h, CH * j:CH * (j + 1)], start=True, stop=True),
                                waits=(("sP", exp_hist[gi - 2] if gi >= 2 else 0),
                                       ("sYE", yev_at[j % 2] if tk == 0 else 0),
                                       ("sRW", rsev_at[j % 2] if tk == 0 else 0)),
                                inc="sS")
                            P.op("scalar", lambda s, tk=tk, h=h: s.activation(
                                out=pexp[:, tk % 4, :], in_=ps[tk % 2][:],
                                func=AF.Exp, bias=colb[:, 36 + h % 2:37 + h % 2], scale=ISD),
                                waits=(("sS", P.n("sS")), ("sG", negb_rdy),
                                       ("sAV", av_hist[gi - 4] if gi >= 4 else 0),
                                       ("sAUX", masks_rdy)),
                                inc="sP")
                            if tk >= ntk - 4:
                                rel = tk - (ntk - 4)
                                P.op("vector", lambda v, tk=tk, rel=rel: v.tensor_mul(
                                    out=pexp[:, tk % 4, :], in0=pexp[:, tk % 4, :],
                                    in1=masks[:, rel, :]),
                                    waits=(("sP", P.n("sP")),), inc="sP")
                            exp_hist.append(P.n("sP"))
                            P.op("tensor", lambda t, tk=tk, ntk=ntk, prssel=prssel: t.matmul(
                                out=prssel[:], lhsT=onesc[:], rhs=pexp[:, tk % 4, :],
                                start=(tk == 0), stop=(tk == ntk - 1)),
                                waits=(("sP", exp_hist[-1]),))
                            P.op("tensor", lambda t, h=h, tk=tk, ntk=ntk, pysel=pysel: t.matmul(
                                out=pysel[:], lhsT=vbuf[:, tk, 128 * h:128 * (h + 1)],
                                rhs=pexp[:, tk % 4, :],
                                start=(tk == 0), stop=(tk == ntk - 1)),
                                inc="sAV")
                            av_hist.append(P.n("sAV"))
                        av_done = P.n("sAV")
                        P.op("vector", lambda v, prssel=prssel: v.tensor_copy(
                            out=row(R_T1), in_=prssel[:]),
                            waits=(("sAV", av_done),), inc="sRW")
                        rsev_at[j % 2] = P.n("sRW")
                        P.op("vector", lambda v: v.reciprocal(out=row(R_T2), in_=row(R_T1)))
                        P.op("vector", lambda v: v.tensor_copy(out=row16(0), in_=row(R_T2)),
                             waits=(("sSM", P.n("sSM")),), inc="sRW")
                        P.op("tensor", lambda t: t.matmul(
                            out=pinv[:], lhsT=onesr[:], rhs=row16(0), start=True, stop=True),
                            waits=(("sRW", P.n("sRW")),), inc="sSM")
                        P.op("scalar", lambda s, j=j: s.activation(
                            out=invB[:, j % 2, :], in_=pinv[:], func=AF.Copy, bias=0.0,
                            scale=1.0),
                            waits=(("sSM", P.n("sSM")), ("sYE", yev_at[j % 2])),
                            inc="sBC")
                        P.op("vector", lambda v, h=h, j=j, pysel=pysel: v.scalar_tensor_tensor(
                            out=yh[:, h % 2, CH * j:CH * (j + 1)], in0=pysel[:], scalar=1.0,
                            in1=invB[:, j % 2, :], op0=ALU.mult, op1=ALU.mult),
                            waits=(("sBC", P.n("sBC")), ("sAGS", agdma_at[h % 2])),
                            inc="sYE")
                        yev_at[j % 2] = P.n("sYE")
                    exp_at_head[h] = P.n("sP")
                    if h + 1 < NHC:
                        emit_bound(h + 1)
                    P.op("gpsimd", lambda g, h=h: g.dma_start(
                        out=ag_in[h].ap(), in_=yh[:, h % 2, :]),
                        waits=(("sYE", P.n("sYE")),), inc="sAGS", inc_k=16)
                    agdma_at[h % 2] = P.n("sAGS")
                    P.op("gpsimd", lambda g, h=h: g.collective_compute(
                        "AllGather", ALU.bypass, RG, ins=[ag_in[h].ap()], outs=[ag_out[h].ap()]),
                        waits=(("sAGS", P.n("sAGS")),), inc="sAGD")
                    if "y" in taps:
                        P.op("sync", lambda s, h=h: s.dma_start(
                            out=dbg["y"][:, h, :], in_=yh[:, h % 2, :]),
                            waits=(("sYE", P.n("sYE")),), inc="sST", inc_k=16)
                P.barrier()

        if _PHASES < 3:
            with nc.Block() as blk:
                P.replay(blk)
            return nc
        with ExitStack() as S34:
            e34 = S34.enter_context
            xq = e34(nc.sbuf_tensor("xq", [128, NCI, 1024], BF16))
            gxB = e34(nc.sbuf_tensor("gxB", [128, 4, CH], F32))
            scGV = e34(nc.sbuf_tensor("scGV", [128, 4, CH], F32))

            with ExitStack() as Cx:
                ec = Cx.enter_context
                wpb = ec(nc.sbuf_tensor("wpb", [128, NCI, D], BF16))
                recv = ec(nc.sbuf_tensor("recv", [128, 32, CH], BF16))
                wpst = ec(nc.sbuf_tensor("wpst", [128, 2, 1024], F32))
                xrst = ec(nc.sbuf_tensor("xrst", [128, 2, CH], F32))
                r1t = ec(nc.sbuf_tensor("r1t", [128, 2, CH], F32))
                s16 = ec(nc.sbuf_tensor("s16", [128, 2, CH], BF16))
                r116 = ec(nc.sbuf_tensor("r116", [128, 2, CH], BF16))
                mmx = ec(nc.sbuf_tensor("mmx", [128, 2, CH], F32))
                mmn = ec(nc.sbuf_tensor("mmn", [128, 2, CH], F32))
                qt1 = ec(nc.sbuf_tensor("qt1", [128, CH], F32))
                qt2 = ec(nc.sbuf_tensor("qt2", [128, CH], FP16))
                qrd = ec(nc.sbuf_tensor("qrd", [128, 2, CH], F32))
                ppj = [ec(nc.psum_tensor(f"ppj{i}", [128, CH], F32)) for i in range(4)]
                pl1 = ec(nc.psum_tensor("pl1", [1, CH], F32))
                pl2 = ec(nc.psum_tensor("pl2", [1, CH], F32))

                P.op("scalar", lambda s: s.preload_act_table(AF.Sqrt))
                for ci in range(NCI):
                    for hz in range(2):
                        kk = 2 * ci + hz
                        dsem = "dWP%d" % (kk % 2)
                        P.op("sync", lambda s, ci=ci, hz=hz, kk=kk: s.dma_start(
                            out=wpst[:, kk % 2, :], in_=wpT[ci][:, 1024 * hz:1024 * (hz + 1)]),
                            waits=(("sWP", kk - 1),), inc=dsem, inc_k=16)
                        P.op("scalar", lambda s, ci=ci, hz=hz, kk=kk: s.activation(
                            out=wpb[:, ci, 1024 * hz:1024 * (hz + 1)], in_=wpst[:, kk % 2, :],
                            func=AF.Copy, bias=0.0, scale=1.0),
                            waits=((dsem, P.n(dsem)),), inc="sWP")
                wp_rdy = P.n("sWP")

                for h in range(NHC):
                    P.op("gpsimd", lambda g, h=h: g.dma_start(
                        out=recv[:, h:32:4, :].rearrange("p r t -> p r t"),
                        in_=ag_out[h].ap()[:, bass.ts(nc.partition_id(), CH)]
                        .rearrange("(r p) t -> p r t", p=128)),
                        waits=(("sAGD", gp_agd + h + 1),), inc="dRC", inc_k=16)
                recv_all = P.n("dRC")

                ridx = 0
                xr_at = {}
                ppj_free = {o: 0 for o in range(4)}
                r1sp_at = {}
                l2t_hist = []
                for bb in range(2):
                    for oti in range(NCI):
                        ri = NCI * bb + oti
                        dsem = "dXR%d" % (ri % 2)
                        P.op("sync", lambda s, bb=bb, oti=oti, ri=ri: s.dma_start(
                            out=xrst[:, ri % 2, :], in_=xresi[bb, oti]),
                            waits=(("sPJ", ri - 1),), inc=dsem, inc_k=16)
                        xr_at[(bb, oti)] = (dsem, P.n(dsem))
                    for og in range(4):
                        for ci in range(NCI):
                            for o in range(4):
                                oti = 4 * og + o
                                P.op("tensor", lambda t, bb=bb, ci=ci, o=o, oti=oti: t.matmul(
                                    out=ppj[o][:], lhsT=wpb[:, ci, 128 * oti:128 * (oti + 1)],
                                    rhs=recv[:, 16 * bb + ci, :],
                                    start=(ci == 0), stop=(ci == NCI - 1)),
                                    waits=(("sWP", wp_rdy), ("dRC", recv_all),
                                           ("sPJ", ppj_free[o] if ci == 0 else 0)),
                                    inc=("sQK" if ci == NCI - 1 else None))
                        grp_done = P.n("sQK")
                        for o in range(4):
                            oti = 4 * og + o
                            P.op("vector", lambda v, o=o, ridx=ridx: v.tensor_add(
                                out=r1t[:, ridx % 2, :], in0=ppj[o][:], in1=xrst[:, ridx % 2, :]),
                                waits=(("sQK", grp_done - 3 + o), xr_at[(bb, oti)],
                                       ("dR1%d" % (ridx % 2), r1sp_at.get(ridx - 2, 0))),
                                inc="sPJ")
                            ppj_free[o] = P.n("sPJ")
                            r1_rdy = P.n("sPJ")
                            P.op("scalar", lambda s, ridx=ridx: s.activation(
                                out=r116[:, ridx % 2, :], in_=r1t[:, ridx % 2, :],
                                func=AF.Copy, bias=0.0, scale=1.0),
                                waits=(("sPJ", r1_rdy),
                                       ("sL2T", l2t_hist[ridx - 2] if ridx >= 2 else 0)))
                            P.op("scalar", lambda s, ridx=ridx: s.activation(
                                out=s16[:, ridx % 2, :], in_=r1t[:, ridx % 2, :],
                                func=AF.Square, bias=0.0, scale=1.0),
                                inc="sL2")
                            st_rdy = P.n("sL2")
                            first = (og == 0 and o == 0)
                            last = (og == 3 and o == 3)
                            P.op("tensor", lambda t, ridx=ridx, first=first, last=last: t.matmul(
                                out=pl1[:], lhsT=onesc[:], rhs=r116[:, ridx % 2, :],
                                start=first, stop=last),
                                waits=(("sL2", st_rdy),
                                       ("sRW", stat_free if first else 0)))
                            P.op("tensor", lambda t, ridx=ridx, first=first, last=last: t.matmul(
                                out=pl2[:], lhsT=onesc[:], rhs=s16[:, ridx % 2, :],
                                start=first, stop=last),
                                inc="sL2T")
                            l2t_hist.append(P.n("sL2T"))
                            if first:
                                P.op("vector", lambda v, bb=bb, ridx=ridx: v.tensor_copy(
                                    out=mmx[:, bb, :], in_=r1t[:, ridx % 2, :]),
                                    waits=(("sPJ", r1_rdy),))
                                P.op("vector", lambda v, bb=bb, ridx=ridx: v.tensor_copy(
                                    out=mmn[:, bb, :], in_=r1t[:, ridx % 2, :]), inc="sQZ")
                            else:
                                P.op("vector", lambda v, bb=bb, ridx=ridx: v.tensor_tensor(
                                    out=mmx[:, bb, :], in0=mmx[:, bb, :],
                                    in1=r1t[:, ridx % 2, :], op=ALU.max),
                                    waits=(("sPJ", r1_rdy),))
                                P.op("vector", lambda v, bb=bb, ridx=ridx: v.tensor_tensor(
                                    out=mmn[:, bb, :], in0=mmn[:, bb, :],
                                    in1=r1t[:, ridx % 2, :], op=ALU.min), inc="sQZ")
                            dsem = "dR1%d" % (ridx % 2)
                            P.op("gpsimd", lambda g, bb=bb, oti=oti, ridx=ridx: g.dma_start(
                                out=res1_d[bb, oti], in_=r1t[:, ridx % 2, :]),
                                waits=(("sQZ", P.n("sQZ")),), inc=dsem, inc_k=16)
                            r1sp_at[ridx] = P.n(dsem)
                            ridx += 1
                    l2_done = P.n("sL2T")
                    mm_done = P.n("sQZ")
                    P.op("vector", lambda v, bb=bb: v.tensor_scalar(
                        out=mmn[:, bb, :], in0=mmn[:, bb, :], scalar1=-1.0, scalar2=None,
                        op0=ALU.mult),
                        waits=(("sQZ", mm_done),), inc="sQZ")
                    P.op("gpsimd", lambda g, bb=bb: g.partition_all_reduce(
                        mmx[:, bb, :], mmx[:, bb, :], 128, ReduceOp.max),
                        waits=(("sQZ", P.n("sQZ")),))
                    P.op("gpsimd", lambda g, bb=bb: g.partition_all_reduce(
                        mmn[:, bb, :], mmn[:, bb, :], 128, ReduceOp.max),
                        inc="sG")
                    par_done = P.n("sG")
                    P.op("vector", lambda v: v.tensor_scalar(
                        out=row(R_MEAN), in0=pl1[:], scalar1=1.0 / D, scalar2=None,
                        op0=ALU.mult),
                        waits=(("sL2T", l2_done), ("sG", par_done)))
                    P.op("vector", lambda v: v.tensor_scalar(
                        out=row(R_EX2), in0=pl2[:], scalar1=1.0 / D, scalar2=None,
                        op0=ALU.mult), inc="sRW")
                    stat_free = P.n("sRW")
                    P.op("vector", lambda v: v.tensor_mul(out=row(R_T1), in0=row(R_MEAN),
                                                          in1=row(R_MEAN)))
                    P.op("vector", lambda v: v.tensor_sub(out=row(R_EX2), in0=row(R_EX2),
                                                          in1=row(R_T1)), inc="sRW")
                    P.op("vector", lambda v: v.tensor_scalar(
                        out=row(R_T1), in0=row(R_EX2), scalar1=EPS, scalar2=None,
                        op0=ALU.add))
                    P.op("vector", lambda v: v.reciprocal(out=row(R_T1), in_=row(R_T1)),
                         inc="sRW")
                    P.op("scalar", lambda s: s.activation(out=row(R_R), in_=row(R_T1),
                                                          func=AF.Sqrt, bias=0.0, scale=1.0),
                         waits=(("sRW", P.n("sRW")),), inc="sBC")
                    P.op("vector", lambda v, bb=bb: v.tensor_sub(
                        out=row(R_T1), in0=mmx[0:1, bb, :], in1=row(R_MEAN)),
                        waits=(("sBC", P.n("sBC")),))
                    P.op("vector", lambda v, bb=bb: v.tensor_add(
                        out=row(R_T2), in0=row(R_MEAN), in1=mmn[0:1, bb, :]))
                    P.op("vector", lambda v: v.tensor_tensor(
                        out=row(R_T1), in0=row(R_T1), in1=row(R_T2), op=ALU.max))
                    P.op("vector", lambda v: v.tensor_mul(out=row(R_T2), in0=row(R_T1),
                                                          in1=row(R_R)))
                    P.op("vector", lambda v: v.tensor_scalar(
                        out=row(R_T2), in0=row(R_T2), scalar1=EPS, scalar2=None, op0=ALU.max))
                    P.op("vector", lambda v: v.reciprocal(out=row(R_T3), in_=row(R_T2)))
                    P.op("vector", lambda v: v.tensor_mul(out=row(R_EX2), in0=row(R_T2),
                                                          in1=row(R_T3)))
                    P.op("vector", lambda v: v.tensor_scalar(
                        out=row(R_EX2), in0=row(R_EX2), scalar1=-1.0, scalar2=2.0,
                        op0=ALU.mult, op1=ALU.add))
                    P.op("vector", lambda v: v.tensor_mul(out=row(R_T3), in0=row(R_T3),
                                                          in1=row(R_EX2)))
                    P.op("vector", lambda v: v.tensor_mul(out=row(R_T1), in0=row(R_R),
                                                          in1=row(R_T3)))
                    P.op("vector", lambda v: v.tensor_scalar(
                        out=row(R_T1), in0=row(R_T1), scalar1=127.0, scalar2=None,
                        op0=ALU.mult))
                    P.op("vector", lambda v: v.tensor_mul(out=row(R_T3), in0=row(R_MEAN),
                                                          in1=row(R_T1)))
                    P.op("vector", lambda v: v.tensor_scalar(
                        out=row(R_T3), in0=row(R_T3), scalar1=-1.0, scalar2=MAGIC,
                        op0=ALU.mult, op1=ALU.add))
                    P.op("vector", lambda v: v.tensor_scalar(
                        out=row(R_MR), in0=row(R_T2),
                        scalar1=rp[0:1, R_GAM + 0:R_GAM + 1], scalar2=1.0 / 127.0,
                        op0=ALU.mult, op1=ALU.mult),
                        waits=(("sGP", gam_rdy),))
                    P.op("vector", lambda v: v.tensor_scalar(
                        out=row(R_T2), in0=row(R_T2),
                        scalar1=rp[0:1, R_GAM + 1:R_GAM + 2], scalar2=1.0 / 127.0,
                        op0=ALU.mult, op1=ALU.mult), inc="sRW")
                    qrows_rdy = P.n("sRW")
                    P.op("gpsimd", lambda g, bb=bb: g.partition_broadcast(
                        gxB[:, 2 * bb + 0, :], rp[0:1, R_T1:R_T1 + CH], 128),
                        waits=(("sRW", qrows_rdy),))
                    P.op("gpsimd", lambda g, bb=bb: g.partition_broadcast(
                        gxB[:, 2 * bb + 1, :], rp[0:1, R_T3:R_T3 + CH], 128))
                    P.op("gpsimd", lambda g, bb=bb: g.partition_broadcast(
                        scGV[:, 0 + bb, :], rp[0:1, R_MR:R_MR + CH], 128))
                    P.op("gpsimd", lambda g, bb=bb: g.partition_broadcast(
                        scGV[:, 2 + bb, :], rp[0:1, R_T2:R_T2 + CH], 128), inc="sG")
                    qb_rdy = P.n("sG")
                    if "xq" in taps and bb == 0:
                        P.wait("sync", "sRW", qrows_rdy)
                        P.op("sync", lambda s: s.dma_start(
                            out=dbg["ab"][0:1, 0:CH], in_=row(R_T1)), inc="sST", inc_k=16)
                        P.op("sync", lambda s: s.dma_start(
                            out=dbg["ab"][0:1, CH:2 * CH], in_=row(R_T3)), inc="sST", inc_k=16)

                    qz_hist = {}
                    for ci in range(NCI):
                        dsem = "dQR%d" % (ci % 2)
                        P.op("sync", lambda s, bb=bb, ci=ci: s.dma_start(
                            out=qrd[:, ci % 2, :], in_=res1_d[bb, ci]),
                            waits=(("dR10", P.n("dR10")), ("dR11", P.n("dR11")),
                                   ("sQZ", qz_hist.get(ci - 2, 0))),
                            inc=dsem, inc_k=16)
                        xr_now = (dsem, P.n(dsem))
                        P.op("vector", lambda v, ci=ci, bb=bb: v.scalar_tensor_tensor(
                            out=qt1[:], in0=qrd[:, ci % 2, :], scalar=1.0,
                            in1=gxB[:, 2 * bb + 0, :], op0=ALU.mult, op1=ALU.mult),
                            waits=(xr_now, ("sG", qb_rdy)))
                        P.op("vector", lambda v, ci=ci, bb=bb: v.tensor_add(
                            out=qt2[:], in0=qt1[:], in1=gxB[:, 2 * bb + 1, :]))
                        P.op("vector", lambda v: v.tensor_scalar(
                            out=qt2[:], in0=qt2[:], scalar1=MAGIC - 128.0,
                            scalar2=MAGIC + 127.0, op0=ALU.max, op1=ALU.min))
                        P.op("vector", lambda v, ci=ci, bb=bb: v.tensor_scalar(
                            out=xq[:, ci, CH * bb:CH * (bb + 1)], in0=qt2[:],
                            scalar1=-MAGIC, scalar2=None, op0=ALU.add),
                            inc="sQZ")
                        qz_hist[ci] = P.n("sQZ")
                if "res1" in taps:
                    P.wait("sync", "dR10", P.n("dR10"))
                    P.wait("sync", "dR11", P.n("dR11"))
                    P.op("sync", lambda s: s.dma_start(out=dbg["res1"], in_=res1_d), inc="sST", inc_k=16)
                if "xq" in taps:
                    P.wait("sync", "sQZ", P.n("sQZ"))
                    P.op("sync", lambda s: s.dma_start(out=dbg["xq"], in_=xq[:]), inc="sST", inc_k=16)
                P.barrier()

            if _PHASES < 4:
                with nc.Block() as blk:
                    P.replay(blk)
                return nc
            with ExitStack() as Dx:
                ed = Dx.enter_context
                terng = ed(nc.sbuf_tensor("terng", [128, 2, NCI, 256], BF16))
                ternv = ed(nc.sbuf_tensor("ternv", [128, 2, NCI, 256], BF16))
                gwst = ed(nc.sbuf_tensor("gwst", [128, 4, 256], F32))
                tgf = ed(nc.sbuf_tensor("tgf", [128, 2, 256], FP16))
                tgf2 = ed(nc.sbuf_tensor("tgf2", [128, 2, 256], FP16))
                owst = ed(nc.sbuf_tensor("owst", [128, 2, 1024], F32))
                owf = ed(nc.sbuf_tensor("owf", [128, 1024], FP16))
                owf2 = ed(nc.sbuf_tensor("owf2", [128, 1024], FP16))
                ow8b = ed(nc.sbuf_tensor("ow8b", [128, 1024], FP8))
                e1b = ed(nc.sbuf_tensor("e1b", [128, CH], F32))
                e2b = ed(nc.sbuf_tensor("e2b", [128, CH], F32))
                g16 = ed(nc.sbuf_tensor("g16", [128, CH], F32))
                v16 = ed(nc.sbuf_tensor("v16", [128, CH], F32))
                gvst = ed(nc.sbuf_tensor("gvst", [128, 2, CH], FP16))
                mmgx = ed(nc.sbuf_tensor("mmgx", [128, 2, CH], F32))
                mmgn = ed(nc.sbuf_tensor("mmgn", [128, 2, CH], F32))
                pgv = [ed(nc.psum_tensor(f"pgv{i}", [128, CH], F32)) for i in range(8)]

                P.op("scalar", lambda s: s.preload_act_table(AF.Silu))
                P.op("vector", lambda v: v.memset(mmgx[:], 0.0))
                P.op("vector", lambda v: v.memset(mmgn[:], 0.0), inc="sQZ")
                mmgx_init = P.n("sQZ")

                HC2 = 32
                tern_rdy = {}
                mm_at = {}
                gdma = 0
                ow_i = 0
                ow_dma_at = {}
                ow8_spill_at = {}
                gv_spill = 0
                gvsp_at = {}
                pgv_free = {i: 0 for i in range(8)}
                for hc2 in range(HC2):
                    hcs, half = hc2 // 2, hc2 % 2
                    par = hc2 % 2
                    for wsel in range(2):
                        srcw = gwT if wsel == 0 else vwT
                        dstt = terng if wsel == 0 else ternv
                        for ci in range(NCI):
                            gdma += 1
                            dsem = "dGW%d" % ((gdma - 1) % 4)
                            P.op("sync", lambda s, hcs=hcs, half=half, ci=ci, wsel=wsel,
                                 gdma=gdma, srcw=srcw: s.dma_start(
                                     out=gwst[:, (gdma - 1) % 4, :],
                                     in_=srcw[hcs, ci][:, 256 * half:256 * (half + 1)]),
                                waits=(("sFT", gdma - 4),), inc=dsem, inc_k=16)
                            P.op("vector", lambda v, gdma=gdma, wsel=wsel: v.tensor_scalar(
                                out=tgf2[:, (gdma - 1) % 2, :], in0=gwst[:, (gdma - 1) % 4, :],
                                scalar1=colb[:, 44 + wsel:45 + wsel], scalar2=None,
                                op0=ALU.is_lt),
                                waits=((dsem, P.n(dsem)), ("sG", invg_rdy),
                                       ("sFT", gdma - 2)), inc="sP2")
                            P.wait("vector", "sP2", P.n("sP2"))
                            P.op("vector", lambda v, par=par, ci=ci, gdma=gdma, wsel=wsel,
                                 dstt=dstt: v.scalar_tensor_tensor(
                                out=dstt[:, par, ci, :], in0=gwst[:, (gdma - 1) % 4, :],
                                scalar=colb[:, 40 + wsel:41 + wsel],
                                in1=tgf2[:, (gdma - 1) % 2, :],
                                op0=ALU.is_gt, op1=ALU.subtract),
                                waits=(("sFM", mm_at.get(hc2 - 2, 0)),),
                                inc="sFT")
                    tern_rdy[hc2] = P.n("sFT")
                    for ht in range(2):
                        hi = 2 * hc2 + ht
                        for wsel in range(2):
                            srct = terng if wsel == 0 else ternv
                            for ci in range(NCI):
                                for tc in range(2):
                                    bank = 4 * (hi % 2) + 2 * wsel + tc
                                    P.op("tensor", lambda t, par=par, ci=ci, ht=ht, tc=tc,
                                         bank=bank, srct=srct: t.matmul(
                                             out=pgv[bank][:],
                                             lhsT=srct[:, par, ci, 128 * ht:128 * (ht + 1)],
                                             rhs=xq[:, ci, CH * tc:CH * (tc + 1)],
                                             start=(ci == 0), stop=(ci == NCI - 1)),
                                        waits=(("sFT", tern_rdy[hc2]),
                                               ("sFE", pgv_free[bank] if ci == 0 else 0)),
                                        inc=("sFM" if ci == NCI - 1 else None))
                        mm_done = P.n("sFM")
                        mm_at[hc2] = mm_done
                        for tc in range(2):
                            bg = 4 * (hi % 2) + tc
                            bv = 4 * (hi % 2) + 2 + tc
                            P.op("vector", lambda v, bg=bg, tc=tc: v.scalar_tensor_tensor(
                                out=e1b[:], in0=pgv[bg][:], scalar=1.0,
                                in1=scGV[:, 0 + tc, :], op0=ALU.mult, op1=ALU.mult),
                                waits=(("sFM", mm_done),), inc="sQK")
                            P.op("scalar", lambda s, hi=hi: s.activation(
                                out=g16[:], in_=e1b[:], func=AF.Silu,
                                bias=gbc[:, hi:hi + 1], scale=1.0),
                                waits=(("sQK", P.n("sQK")),), inc="sBC")
                            P.op("vector", lambda v, bv=bv, tc=tc: v.scalar_tensor_tensor(
                                out=e2b[:], in0=pgv[bv][:], scalar=1.0,
                                in1=scGV[:, 2 + tc, :], op0=ALU.mult, op1=ALU.mult))
                            P.op("vector", lambda v, hi=hi: v.tensor_scalar(
                                out=v16[:], in0=e2b[:], scalar1=vbc[:, hi:hi + 1],
                                scalar2=None, op0=ALU.add), inc="sFE")
                            pgv_free[bg] = P.n("sFE")
                            pgv_free[bv] = P.n("sFE")
                            slot = (2 * hi + tc) % 2
                            P.op("vector", lambda v, slot=slot: v.tensor_mul(
                                out=gvst[:, slot, :], in0=g16[:], in1=v16[:]),
                                waits=(("dGV%d" % slot, gvsp_at.get(gv_spill - 2, 0)),
                                       ("sBC", P.n("sBC"))), inc="sV2")
                            P.wait("vector", "sV2", P.n("sV2"))
                            P.op("vector", lambda v, tc=tc, slot=slot: v.tensor_tensor(
                                out=mmgx[:, tc, :], in0=mmgx[:, tc, :], in1=gvst[:, slot, :],
                                op=ALU.max),
                                waits=(("sQZ", mmgx_init),))
                            P.op("vector", lambda v, tc=tc, slot=slot: v.tensor_tensor(
                                out=mmgn[:, tc, :], in0=mmgn[:, tc, :], in1=gvst[:, slot, :],
                                op=ALU.min), inc="sFE")
                            gv_rdy = P.n("sFE")
                            P.op("gpsimd", lambda g, hi=hi, tc=tc, slot=slot: g.dma_start(
                                out=gv_d[hi][:, CH * tc:CH * (tc + 1)], in_=gvst[:, slot, :]),
                                waits=(("sFE", gv_rdy),), inc="dGV%d" % slot, inc_k=16)
                            gvsp_at[gv_spill] = P.n("dGV%d" % slot)
                            gv_spill += 1
                    for _ in range(2):
                        hio = ow_i
                        ow_i += 1
                        for hz in range(2):
                            kk = 2 * hio + hz
                            dsem = "dOW%d" % (kk % 2)
                            P.op("sync", lambda s, hio=hio, hz=hz, kk=kk: s.dma_start(
                                out=owst[:, kk % 2, :], in_=owTt[hio][:, 1024 * hz:1024 * (hz + 1)]),
                                waits=(("sO8", ow_dma_at.get(kk - 2, 0)),),
                                inc=dsem, inc_k=16)
                            P.op("vector", lambda v, kk=kk: v.tensor_scalar(
                                out=owf2[:], in0=owst[:, kk % 2, :],
                                scalar1=colb[:, 46:47], scalar2=None,
                                op0=ALU.is_lt),
                                waits=((dsem, P.n(dsem)), ("sG", invg_rdy),
                                       ("dO8A", ow8_spill_at.get(kk - 1, 0))), inc="sP2")
                            P.wait("vector", "sP2", P.n("sP2"))
                            P.op("vector", lambda v, kk=kk: v.scalar_tensor_tensor(
                                out=ow8b[:], in0=owst[:, kk % 2, :],
                                scalar=colb[:, 42:43], in1=owf2[:],
                                op0=ALU.is_gt, op1=ALU.subtract), inc="sO8")
                            ow_dma_at[kk] = P.n("sO8")
                            P.op("gpsimd", lambda g, hio=hio, hz=hz: g.dma_start(
                                out=ow8_d[hio][:, 1024 * hz:1024 * (hz + 1)], in_=ow8b[:]),
                                waits=(("sO8", P.n("sO8")),), inc="dO8A", inc_k=16)
                            ow8_spill_at[kk] = P.n("dO8A")
                if "gv" in taps:
                    P.wait("sync", "dGV0", P.n("dGV0"))
                    P.wait("sync", "dGV1", P.n("dGV1"))
                    P.op("sync", lambda s: s.dma_start(out=dbg["gv"], in_=gv_d), inc="sST", inc_k=16)
                    P.wait("sync", "sFT", P.n("sFT"))
                    P.op("sync", lambda s: s.dma_start(out=dbg["tern"][0], in_=terng[:]), inc="sST", inc_k=16)
                    P.op("sync", lambda s: s.dma_start(out=dbg["tern"][1], in_=ternv[:]), inc="sST", inc_k=16)

                P.op("vector", lambda v: v.tensor_scalar(
                    out=mmgn[:, 0, :], in0=mmgn[:, 0, :], scalar1=-1.0, scalar2=None,
                    op0=ALU.mult),
                    waits=(("sFE", P.n("sFE")),))
                P.op("vector", lambda v: v.tensor_scalar(
                    out=mmgn[:, 1, :], in0=mmgn[:, 1, :], scalar1=-1.0, scalar2=None,
                    op0=ALU.mult), inc="sQZ")
                P.op("gpsimd", lambda g: g.partition_all_reduce(
                    mmgx[:, 0, :], mmgx[:, 0, :], 128, ReduceOp.max),
                    waits=(("sQZ", P.n("sQZ")),))
                P.op("gpsimd", lambda g: g.partition_all_reduce(
                    mmgx[:, 1, :], mmgx[:, 1, :], 128, ReduceOp.max))
                P.op("gpsimd", lambda g: g.partition_all_reduce(
                    mmgn[:, 0, :], mmgn[:, 0, :], 128, ReduceOp.max))
                P.op("gpsimd", lambda g: g.partition_all_reduce(
                    mmgn[:, 1, :], mmgn[:, 1, :], 128, ReduceOp.max), inc="sG")
                P.op("vector", lambda v: v.tensor_tensor(
                    out=row(R_T1), in0=mmgx[0:1, 0, :], in1=mmgn[0:1, 0, :], op=ALU.max),
                    waits=(("sG", P.n("sG")),))
                P.op("vector", lambda v: v.tensor_tensor(
                    out=row(R_T2), in0=mmgx[0:1, 1, :], in1=mmgn[0:1, 1, :], op=ALU.max))
                P.op("vector", lambda v: v.tensor_scalar(
                    out=row(R_T1), in0=row(R_T1), scalar1=EPS, scalar2=None, op0=ALU.max))
                P.op("vector", lambda v: v.tensor_scalar(
                    out=row(R_T2), in0=row(R_T2), scalar1=EPS, scalar2=None, op0=ALU.max))
                for tc, src in ((0, R_T1), (1, R_T2)):
                    P.op("vector", lambda v, tc=tc, src=src: v.tensor_scalar(
                        out=row(R_SCO0 + 512 * tc), in0=row(src),
                        scalar1=rp[0:1, R_GAM + 2:R_GAM + 3], scalar2=1.0 / 127.0,
                        op0=ALU.mult, op1=ALU.mult))
                    P.op("vector", lambda v, tc=tc, src=src: v.reciprocal(
                        out=row(R_QB0 + 512 * tc), in_=row(src)))
                    P.op("vector", lambda v, tc=tc, src=src: v.tensor_mul(
                        out=row(R_T3), in0=row(src), in1=row(R_QB0 + 512 * tc)))
                    P.op("vector", lambda v: v.tensor_scalar(
                        out=row(R_T3), in0=row(R_T3), scalar1=-1.0, scalar2=2.0,
                        op0=ALU.mult, op1=ALU.add))
                    P.op("vector", lambda v, tc=tc: v.tensor_mul(
                        out=row(R_QB0 + 512 * tc), in0=row(R_QB0 + 512 * tc), in1=row(R_T3)))
                    P.op("vector", lambda v, tc=tc: v.tensor_scalar(
                        out=row(R_QB0 + 512 * tc), in0=row(R_QB0 + 512 * tc),
                        scalar1=127.0, scalar2=None, op0=ALU.mult), inc="sRW")
                gvrows_rdy = P.n("sRW")
                P.barrier()

        if _PHASES < 5:
            with nc.Block() as blk:
                P.replay(blk)
            return nc
        with ExitStack() as Ex:
            ee = Ex.enter_context
            gvq = ee(nc.sbuf_tensor("gvq", [128, NHI, 1024], BF16))
            outsb = ee(nc.sbuf_tensor("outsb", [128, 4, 1024], F32))
            gvst2 = ee(nc.sbuf_tensor("gvst2", [128, 2, 1024], FP16))
            o8b = ee(nc.sbuf_tensor("o8b", [128, 2, 1024], FP8))
            scO = ee(nc.sbuf_tensor("scO", [128, 2, CH], F32))
            qbB = ee(nc.sbuf_tensor("qbB", [128, 2, CH], F32))
            r1rb = ee(nc.sbuf_tensor("r1rb", [128, 2, CH], F32))
            qt1e = ee(nc.sbuf_tensor("qt1e", [128, CH], F32))
            qt2e = ee(nc.sbuf_tensor("qt2e", [128, CH], FP16))
            e1e = ee(nc.sbuf_tensor("e1e", [128, CH], F32))
            ofm = ee(nc.sbuf_tensor("ofm", [128, CH], F32))
            po = [ee(nc.psum_tensor(f"po{i}", [128, CH], F32)) for i in range(8)]

            for tc in range(2):
                P.op("gpsimd", lambda g, tc=tc: g.partition_broadcast(
                    scO[:, tc, :], rp[0:1, R_SCO0 + 512 * tc:R_SCO0 + 512 * tc + CH], 128),
                    waits=(("sRW", gvrows_rdy),))
                P.op("gpsimd", lambda g, tc=tc: g.partition_broadcast(
                    qbB[:, tc, :], rp[0:1, R_QB0 + 512 * tc:R_QB0 + 512 * tc + CH], 128),
                    inc="sG")
            scq_rdy = P.n("sG")

            qz_at = {}
            for hi in range(NHI):
                dsem = "dG2%d" % (hi % 2)
                P.op("gpsimd", lambda g, hi=hi: g.dma_start(
                    out=gvst2[:, hi % 2, :], in_=gv_d[hi]),
                    waits=(("dGV0", P.n("dGV0")), ("dGV1", P.n("dGV1")),
                           ("sQZ", qz_at.get(hi - 2, 0))),
                    inc=dsem, inc_k=16)
                gvdma = (dsem, P.n(dsem))
                for tc in range(2):
                    P.op("vector", lambda v, hi=hi, tc=tc: v.scalar_tensor_tensor(
                        out=qt1e[:], in0=gvst2[:, hi % 2, CH * tc:CH * (tc + 1)], scalar=1.0,
                        in1=qbB[:, tc, :], op0=ALU.mult, op1=ALU.mult),
                        waits=(gvdma, ("sG", scq_rdy)))
                    P.op("vector", lambda v: v.tensor_scalar(
                        out=qt2e[:], in0=qt1e[:], scalar1=MAGIC, scalar2=None, op0=ALU.add))
                    P.op("vector", lambda v: v.tensor_scalar(
                        out=qt2e[:], in0=qt2e[:], scalar1=MAGIC - 128.0,
                        scalar2=MAGIC + 127.0, op0=ALU.max, op1=ALU.min))
                    P.op("vector", lambda v, hi=hi, tc=tc: v.tensor_scalar(
                        out=gvq[:, hi, CH * tc:CH * (tc + 1)], in0=qt2e[:], scalar1=-MAGIC,
                        scalar2=None, op0=ALU.add), inc="sQZ")
                qz_at[hi] = P.n("sQZ")
            qz_all = P.n("sQZ")

            rounds = [] if _KP5 < 2 else [(0, 0), (1, 0), (0, 1), (1, 1)]
            rounds = rounds[:int(os.environ.get("KP5R", "4"))]
            pm_at = 0
            o8_at = {}
            oev_at = {}
            st_at = {}
            for rnd, (og, tc) in enumerate(rounds):
                for hi in range(NHI):
                    kk = rnd * NHI + hi
                    dsem = "dOB%d" % (kk % 2)
                    P.op("sync", lambda s, hi=hi, og=og, kk=kk: s.dma_start(
                        out=o8b[:, kk % 2, :], in_=ow8_d[hi][:, 1024 * og:1024 * (og + 1)]),
                        waits=(("dO8A", P.n("dO8A")),
                               ("sFM", o8_at.get(kk - 2, 0))),
                        inc=dsem, inc_k=16)
                    o8_now = (dsem, P.n(dsem))
                    for o in range(8):
                        oti = 8 * og + o
                        P.op("tensor", lambda t, hi=hi, o=o, tc=tc, kk=kk: t.matmul(
                            out=po[o][:], lhsT=o8b[:, kk % 2, 128 * o:128 * (o + 1)],
                            rhs=gvq[:, hi, CH * tc:CH * (tc + 1)],
                            start=(hi == 0), stop=(hi == NHI - 1)),
                            waits=(o8_now,
                                   ("sQZ", qz_at[hi] if rnd == 0 else 0),
                                   ("sOE", oev_at.get((rnd - 1, o), 0) if hi == 0 else 0)),
                            inc=("sFM" if o == 7 else None))
                    o8_at[kk] = P.n("sFM")
                mmo_done = P.n("sFM")
                for o in range(8):
                    oti = 8 * og + o
                    dsem = "dRB%d" % (o % 2)
                    P.op("sync", lambda s, tc=tc, oti=oti, o=o, rnd=rnd: s.dma_start(
                        out=r1rb[:, o % 2, :], in_=res1_d[tc, oti]),
                        waits=(("sOE", oev_at.get((rnd, o - 2), 0)),
                               ("dR10", P.n("dR10")), ("dR11", P.n("dR11"))),
                        inc=dsem, inc_k=16)
                    r1_now = (dsem, P.n(dsem))
                    if os.environ.get("KP5E"):
                        continue
                    P.op("vector", lambda v, o=o, tc=tc: v.scalar_tensor_tensor(
                        out=e1e[:], in0=po[o][:], scalar=1.0, in1=scO[:, tc, :],
                        op0=ALU.mult, op1=ALU.mult),
                        waits=(("sFM", mmo_done), ("sG", scq_rdy), r1_now))
                    P.op("vector", lambda v, o=o, oti=oti: v.scalar_tensor_tensor(
                        out=ofm[:], in0=e1e[:], scalar=obc[:, oti:oti + 1], in1=r1rb[:, o % 2, :],
                        op0=ALU.add, op1=ALU.add),
                        waits=(("sPM", pm_at),), inc="sEV")
                    ofm_rdy = P.n("sEV")
                    for q in range(4) if _KP5 >= 3 else []:
                        P.op("tensor", lambda t, o=o, q=q: t.transpose(
                            out=po[o][:, 128 * q:128 * (q + 1)],
                            in_=ofm[:, 128 * q:128 * (q + 1)], identity=identf[:]),
                            waits=(("sEV", ofm_rdy), ("sOE", P.n("sOE"))),
                            inc="sPM")
                        P.op("scalar", lambda s, o=o, q=q: s.activation(
                            out=outsb[:, q, 128 * o:128 * (o + 1)],
                            in_=po[o][:, 128 * q:128 * (q + 1)], func=AF.Copy,
                            bias=0.0, scale=1.0),
                            waits=(("sPM", P.n("sPM")),
                                   ("sST", st_at.get(rnd - 1, 0))),
                            inc="sOE")
                    oev_at[(rnd, o)] = P.n("sOE")
                    pm_at = P.n("sPM")
                for q in range(4) if _KP5 >= 4 else []:
                    P.op("sync", lambda s, tc=tc, og=og, q=q: s.dma_start(
                        out=out[tc, 128 * q:128 * (q + 1), 1024 * og:1024 * (og + 1)],
                        in_=outsb[:, q, :]),
                        waits=(("sOE", P.n("sOE")),), inc="sST", inc_k=16)
                st_at[rnd] = P.n("sST")
            P.wait("sync", "sST", P.n("sST"))
            P.wait("gpsimd", "sST", P.n("sST"))
            for s in ("dST0", "dST1", "dGT0", "dGT1", "dWP0", "dWP1",
                      "dXR0", "dXR1", "dQR0", "dQR1", "dRB0", "dRB1",
                      "dGW0", "dGW1", "dGW2", "dGW3", "dOW0", "dOW1",
                      "dOB0", "dOB1", "sAUX"):
                P.wait("sync", s, P.n(s))
            for s in ("dRC", "dR10", "dR11", "dGV0", "dGV1", "dO8A",
                      "dG20", "dG21", "sAGS"):
                P.wait("gpsimd", s, P.n(s))

        with nc.Block() as blk:
            P.replay(blk)

    return nc


_NC_CACHE = {}


def _get_nc(taps=()):
    key = tuple(sorted(taps))
    if key not in _NC_CACHE:
        nc = build_core(debug_taps=key)
        mybir.codegen_inst_isa_subclasses(nc)
        _NC_CACHE[key] = nc
    return _NC_CACHE[key]


def make_in_maps(L0_attn_x, L0_attn_W_qkv, L0_attn_W_proj, L0_ffn_gate_w,
                 L0_ffn_gate_b, L0_ffn_val_w, L0_ffn_val_b, L0_ffn_out_w,
                 L0_ffn_out_b):
    x = np.ascontiguousarray(np.asarray(L0_attn_x, dtype=np.float32))
    wqkv = np.asarray(L0_attn_W_qkv, dtype=np.float32)
    wp = np.asarray(L0_attn_W_proj, dtype=np.float32)
    gw = np.asarray(L0_ffn_gate_w, dtype=np.float32)
    gbv = np.asarray(L0_ffn_gate_b, dtype=np.float32)
    vw = np.asarray(L0_ffn_val_w, dtype=np.float32)
    vbv = np.asarray(L0_ffn_val_b, dtype=np.float32)
    ow = np.asarray(L0_ffn_out_w, dtype=np.float32)
    obv = np.asarray(L0_ffn_out_b, dtype=np.float32)

    wpT = np.ascontiguousarray(wp.T).reshape(NCI, 128, D)
    gwTt = np.ascontiguousarray(
        gw.T.reshape(NCI, 128, 16, 512).transpose(2, 0, 1, 3))
    vwTt = np.ascontiguousarray(
        vw.T.reshape(NCI, 128, 16, 512).transpose(2, 0, 1, 3))
    owTt = np.ascontiguousarray(ow.T).reshape(NHI, 128, D)
    gb_t = np.ascontiguousarray(gbv.reshape(NHI, 128).T)
    vb_t = np.ascontiguousarray(vbv.reshape(NHI, 128).T)
    ob_t = np.ascontiguousarray(obv.reshape(NCI, 128).T)

    in_maps = []
    for c in range(8):
        b, g = c // 4, c % 4
        xT = np.ascontiguousarray(x[b].T)
        xTt = np.ascontiguousarray(
            xT.reshape(NCI, 128, NCH, CH).transpose(2, 0, 1, 3))
        xs = x[:, CH * c:CH * (c + 1), :]
        xresi = np.ascontiguousarray(xs.transpose(0, 2, 1)).reshape(2, NCI, 128, CH)
        wsub = np.concatenate([wqkv[512 * g:512 * (g + 1)],
                               wqkv[D + 512 * g:D + 512 * (g + 1)],
                               wqkv[2 * D + 512 * g:2 * D + 512 * (g + 1)]], axis=0)
        wqkvT = np.ascontiguousarray(wsub.T).reshape(NCI, 128, QKVO)
        in_maps.append(dict(
            xTt=xTt, xresi=xresi, wqkvT=wqkvT, wpT=wpT, gwT=gwTt, vwT=vwTt,
            owTt=owTt, gpg=np.ascontiguousarray(gwTt[2 * c:2 * c + 2]),
            gpv=np.ascontiguousarray(vwTt[2 * c:2 * c + 2]),
            gpo=np.ascontiguousarray(owTt[8 * c:8 * c + 8]),
            gb_t=gb_t, vb_t=vb_t, ob_t=ob_t))
    return in_maps


def kernel(**inputs):
    in_maps = make_in_maps(**inputs)
    nc = _get_nc()
    res = run_bass_kernel_spmd(nc, in_maps, core_ids=list(range(8)))
    final = np.zeros((2, T, D), dtype=np.float32)
    for c in range(8):
        oc = np.asarray(res.results[c]["out"])
        final[:, CH * c:CH * (c + 1), :] = oc
    return final
